# version 06
# baseline (speedup 1.0000x reference)
import contextlib
import math

import numpy as np

import concourse.bass as bass
import concourse.mybir as mybir
from concourse.bass_utils import run_bass_kernel_spmd

F32 = mybir.dt.float32
BF16 = mybir.dt.bfloat16
I32 = mybir.dt.int32
AF = mybir.ActivationFunctionType
ALU = mybir.AluOpType

B = 8
L = 2048
D = 2048
H = 32
FD = 64
FF = 8192
NT = L // 128
NDC = D // 128
NSC = 4
NG = 8
NBLK = 4
NHC = FF // 128
LN_EPS = 1e-5
DEN_EPS = 1e-12
SOFF = [0, 512, 896, 1152]
SLEN = 1280


class Sched:

    def __init__(self, nc, sems):
        self.nc = nc
        self.sems = sems
        self.count = {k: 0 for k in sems}
        self.steps = []

    def val(self, sem):
        return self.count[sem]

    log = []

    def add(self, engine, fn, waits=(), inc=None, inc_amt=1):
        w = [(s, v) for (s, v) in waits if v is not None and v > 0]
        if inc is not None:
            self.count[inc] += inc_amt
        self.steps.append((engine, fn, w, inc, inc_amt))
        Sched.log.append((engine, tuple(w), inc, inc_amt))
        return self.count[inc] if inc is not None else None

    def flush(self):
        with self.nc.Block() as blk:
            for name in ("sync", "tensor", "vector", "scalar", "gpsimd"):
                if not any(s[0] == name for s in self.steps):
                    continue

                def make(nm):
                    def body(eng):
                        for (en, fn, waits, inc, amt) in self.steps:
                            if en != nm:
                                continue
                            for (s, v) in waits:
                                eng.wait_ge(self.sems[s], v)
                            ins = fn(eng)
                            if inc is not None:
                                ins.then_inc(self.sems[inc], amt)
                    return body
                getattr(blk, name)(make(name))
        self.steps = []


D16 = 16


def build_program(taps=(), stop_after=5, ngroups=NG, attn_sub='all'):
    nc = bass.Bass("TRN2", target_bir_lowering=False)

    w2tp = nc.dram_tensor("w2tp", [FF, D], F32, kind="ExternalInput").ap()
    wfq = nc.dram_tensor("wfq", [H, 64, FD], F32, kind="ExternalInput").ap()
    wfk = nc.dram_tensor("wfk", [H, 64, FD], F32, kind="ExternalInput").ap()
    x_in = nc.dram_tensor("x_in", [L, D], F32, kind="ExternalInput").ap()
    wkT = nc.dram_tensor("wkT", [D, D], F32, kind="ExternalInput").ap()
    wqT = nc.dram_tensor("wqT", [D, D], F32, kind="ExternalInput").ap()
    wvT = nc.dram_tensor("wvT", [D, D], F32, kind="ExternalInput").ap()
    woT = nc.dram_tensor("woT", [D, D], F32, kind="ExternalInput").ap()
    w1Tt = nc.dram_tensor("w1Tt", [NHC, 128, NDC, 128], F32, kind="ExternalInput").ap()
    w3Tt = nc.dram_tensor("w3Tt", [NHC, 128, NDC, 128], F32, kind="ExternalInput").ap()
    pad_in = nc.dram_tensor("pad_in", [4096, 1024], F32, kind="ExternalInput").ap()
    out = nc.dram_tensor("out", [L, D], F32, kind="ExternalOutput").ap()

    u_dram = nc.dram_tensor("u_scr", [128, 64, 2048], BF16, kind="Internal").ap()
    fl_dram = nc.dram_tensor("fl_scr", [128, 16], F32, kind="Internal").ap()
    w2p = nc.dram_tensor("w2p_scr", [NHC, 128, 2048], BF16, kind="Internal").ap()
    y_dram = nc.dram_tensor("y_scr", [NDC, NT, 128, 128], BF16, kind="Internal").ap()
    x1_dram = nc.dram_tensor("x1_scr", [L, D], F32, kind="Internal").ap()
    ln2_dram = nc.dram_tensor("ln2_scr", [NBLK, NDC, 128, 512], BF16, kind="Internal").ap()

    tap_h = {}

    def tap(name, shape, dtype=F32):
        if name in taps:
            tap_h[name] = nc.dram_tensor("tap_" + name, shape, dtype,
                                         kind="ExternalOutput").ap()
            return tap_h[name]
        return None

    stack = contextlib.ExitStack()
    sems = {n: stack.enter_context(nc.semaphore("sem_" + n))
            for n in ["gW", "sX", "sSt", "PE", "DV", "AC", "GP"]}
    S = Sched(nc, sems)
    _tapped = set()

    def emit_scratch_taps(avail):
        for name, src_ap, shape, dt in avail:
            if name in _tapped:
                continue
            t = tap(name, shape, dt)
            if t is not None:
                _tapped.add(name)
                S.add("sync",
                      (lambda t=t, src_ap=src_ap: lambda sy: sy.dma_start(
                          out=t, in_=src_ap))(),
                      waits=[("sSt", S.val("sSt"))], inc="sSt", inc_amt=D16)

    def sb(st, name, shape, dt=F32):
        return st.enter_context(nc.sbuf_tensor(name, shape, dt))

    def psum(st, name, dt=F32):
        return st.enter_context(nc.psum_tensor(name, [128, 512], dt))

    ident = sb(stack, "ident", [128, 128], BF16)
    mask_tri = sb(stack, "mask_tri", [128, 128], BF16)
    h128 = sb(stack, "h128", [128, 128], BF16)
    h64bd = sb(stack, "h64bd", [128, 128], BF16)
    wfq_bd = sb(stack, "wfq_bd", [128, 16 * 128], BF16)
    wfk_bd = sb(stack, "wfk_bd", [128, 16 * 128], BF16)
    x0_t = sb(stack, "x0_t", [128, D * 2], F32)
    lnst = sb(stack, "lnst", [128, 48], F32)
    lnmv = sb(stack, "lnmv", [128, 4], F32)
    ln_s1 = sb(stack, "ln_s1", [128, 2], F32)
    ln_s2 = sb(stack, "ln_s2", [128, 2], F32)
    eps_sb = sb(stack, "eps_sb", [128, 1], F32)
    flsb = sb(stack, "flsb", [128, 16], F32)

    def flush(queue, sem):
        S.add(queue, lambda e: e.dma_start(out=flsb[:], in_=fl_dram),
              inc=sem, inc_amt=D16)
        return S.val(sem)

    ph01 = contextlib.ExitStack()
    iop = sb(ph01, "iop", [128, 128], I32)
    iof = sb(ph01, "iof", [128, 128], I32)
    it1 = sb(ph01, "it1", [128, 128], I32)
    it2 = sb(ph01, "it2", [128, 128], I32)
    foldi = sb(ph01, "foldi", [128, 2048 * 4], BF16)
    foldo = sb(ph01, "foldo", [128, 2048 * 3], BF16)
    pbank = [psum(ph01, f"pf{i}") for i in range(4)]

    S.add("gpsimd", lambda g: g.iota(iop[:], pattern=[[0, 128]], base=0,
                                     channel_multiplier=1), inc="GP")
    S.add("gpsimd", lambda g: g.iota(iof[:], pattern=[[1, 128]], base=0,
                                     channel_multiplier=0), inc="GP")
    v_io = S.val("GP")
    S.add("vector", lambda v: v.tensor_tensor(ident[:], iop[:], iof[:], ALU.is_equal),
          waits=[("GP", v_io)])
    S.add("vector", lambda v: v.tensor_tensor(mask_tri[:], iof[:], iop[:], ALU.is_ge))
    S.add("vector", lambda v: v.tensor_tensor(it1[:], iop[:], iof[:], ALU.bitwise_and))
    for sh in (4, 2, 1):
        S.add("vector", (lambda sh=sh: lambda v: v.tensor_scalar(
            it2[:], it1[:], sh, None, ALU.arith_shift_right))())
        S.add("vector", lambda v: v.tensor_tensor(it1[:], it1[:], it2[:],
                                                  ALU.bitwise_xor))
    S.add("vector", lambda v: v.tensor_scalar(it2[:], it1[:], 1, None, ALU.bitwise_and))
    S.add("vector", lambda v: v.tensor_scalar(h128[:], it2[:], -2.0, 1.0,
                                              ALU.mult, ALU.add), inc="DV")
    v_h128 = S.val("DV")
    S.add("vector", lambda v: v.memset(h64bd[:], 0.0), inc="DV")
    v_memset = S.val("DV")
    S.add("vector", lambda v: v.tensor_copy(h64bd[0:64, 0:64], h128[0:64, 0:64]),
          waits=[("DV", v_h128), ("DV", v_memset)])
    S.add("vector", lambda v: v.tensor_copy(h64bd[64:128, 64:128], h128[0:64, 0:64]))
    S.add("vector", lambda v: v.memset(eps_sb[:], LN_EPS))
    S.add("vector", lambda v: v.memset(wfq_bd[:], 0.0))
    S.add("vector", lambda v: v.memset(wfk_bd[:], 0.0), inc="DV")
    v_const = S.val("DV")
    for pg in range(16):
        for (dst, srcw) in ((wfq_bd, wfq), (wfk_bd, wfk)):
            S.add("gpsimd",
                  (lambda dst=dst, srcw=srcw, pg=pg: lambda g: g.dma_start(
                      out=dst[0:64, pg * 128:pg * 128 + 64], in_=srcw[2 * pg]))(),
                  waits=[("DV", v_const)] if pg == 0 else [],
                  inc="gW", inc_amt=D16)
            S.add("gpsimd",
                  (lambda dst=dst, srcw=srcw, pg=pg: lambda g: g.dma_start(
                      out=dst[64:128, pg * 128 + 64:pg * 128 + 128],
                      in_=srcw[2 * pg + 1]))(),
                  inc="gW", inc_amt=D16)
    g_wf = flush("gpsimd", "gW")
    dv_wf = S.val("DV")

    fA_fl = {}
    fA_emitted = set()
    slot_mm = [0, 0, 0, 0]
    ev_done = [0, 0, 0, 0]
    st_done = [0, 0, 0]

    def emit_foldA_load(bp):
        if bp in fA_emitted or bp >= 64:
            return
        fA_emitted.add(bp)
        slot = bp % 4
        S.add("gpsimd",
              (lambda bp=bp, slot=slot: lambda g: g.dma_start(
                  out=foldi[:, slot * 2048:(slot + 1) * 2048],
                  in_=w2tp[2 * bp * 64:2 * bp * 64 + 128, :]))(),
              waits=[("PE", slot_mm[slot])], inc="gW", inc_amt=D16)
        if bp % 2 == 1 or bp == 63:
            fv = flush("gpsimd", "gW")
            fA_fl[bp] = fv
            fA_fl.setdefault(bp - 1, fv)
            if fA_fl.get(bp - 1) is None:
                fA_fl[bp - 1] = fv
        else:
            fA_fl[bp] = None

    for bp in range(64):
        for _j in range(bp, bp + 4):
            emit_foldA_load(_j)
        slot = bp % 4
        sloto = bp % 3
        for dob in range(4):
            bank = pbank[dob]
            S.add("tensor",
                  (lambda bank=bank, slot=slot, dob=dob: lambda t: t.matmul(
                      bank[:, 0:512], h64bd[:],
                      foldi[:, slot * 2048 + dob * 512:slot * 2048 + (dob + 1) * 512],
                      start=True, stop=True))(),
                  waits=[("gW", fA_fl[min(bp + 3 - ((bp + 3) + 1) % 2, 63)]
                          if fA_fl[min(bp + 2, 63)] is None
                          else fA_fl[min(bp + 2, 63)]), ("DV", v_const),
                         ("DV", ev_done[dob])],
                  inc="PE")
            mmc = S.val("PE")
            S.add("vector",
                  (lambda bank=bank, sloto=sloto, dob=dob: lambda v: v.tensor_copy(
                      foldo[:, sloto * 2048 + dob * 512:sloto * 2048 + (dob + 1) * 512],
                      bank[:, 0:512]))(),
                  waits=[("PE", mmc), ("sSt", st_done[sloto])], inc="DV")
            ev_done[dob] = S.val("DV")
        slot_mm[slot] = S.val("PE")
        ev4 = S.val("DV")
        S.add("sync",
              (lambda bp=bp, sloto=sloto: lambda sy: sy.dma_start(
                  out=u_dram[2 * bp],
                  in_=foldo[0:64, sloto * 2048:(sloto + 1) * 2048]))(),
              waits=[("DV", ev4)], inc="sSt", inc_amt=D16)
        S.add("sync",
              (lambda bp=bp, sloto=sloto: lambda sy: sy.dma_start(
                  out=u_dram[2 * bp + 1],
                  in_=foldo[64:128, sloto * 2048:(sloto + 1) * 2048]))(),
              inc="sSt", inc_amt=D16)
        S.add("sync", lambda sy: sy.dma_start(out=fl_dram, in_=flsb[:]),
              inc="sSt", inc_amt=D16)
        st_done[sloto] = S.val("sSt")
    stA = S.val("sSt")

    scaleW = 1.0 / math.sqrt(FF)
    fB_fl = {}
    fB_emitted = set()
    slot_mm = [0, 0, 0, 0]
    ev_done = [0, 0, 0, 0]
    st_done = [0, 0, 0]

    def emit_foldB_load(ai):
        if ai in fB_emitted or ai >= 64:
            return
        fB_emitted.add(ai)
        slot = ai % 4
        S.add("gpsimd",
              (lambda ai=ai, slot=slot: lambda g: g.dma_start(
                  out=foldi[:, slot * 2048:(slot + 1) * 2048],
                  in_=u_dram[:, ai, :]))(),
              waits=[("PE", slot_mm[slot])] +
                    ([("sSt", stA)] if ai == 0 else []),
              inc="gW", inc_amt=D16)
        if ai % 2 == 1 or ai == 63:
            fv = flush("gpsimd", "gW")
            fB_fl[ai] = fv
            if fB_fl.get(ai - 1) is None:
                fB_fl[ai - 1] = fv
        else:
            fB_fl[ai] = None

    for ai in range(64):
        for _j in range(ai, ai + 4):
            emit_foldB_load(_j)
        slot = ai % 4
        sloto = ai % 3
        for dob in range(4):
            bank = pbank[dob]
            S.add("tensor",
                  (lambda bank=bank, slot=slot, dob=dob: lambda t: t.matmul(
                      bank[:, 0:512], h128[:],
                      foldi[:, slot * 2048 + dob * 512:slot * 2048 + (dob + 1) * 512],
                      start=True, stop=True))(),
                  waits=[("gW", fB_fl[min(ai + 3 - ((ai + 3) + 1) % 2, 63)]
                          if fB_fl[min(ai + 2, 63)] is None
                          else fB_fl[min(ai + 2, 63)]), ("DV", ev_done[dob])],
                  inc="PE")
            mmc = S.val("PE")
            S.add("vector",
                  (lambda bank=bank, sloto=sloto, dob=dob: lambda v: v.tensor_scalar(
                      foldo[:, sloto * 2048 + dob * 512:sloto * 2048 + (dob + 1) * 512],
                      bank[:, 0:512], scaleW, None, ALU.mult))(),
                  waits=[("PE", mmc), ("sSt", st_done[sloto])], inc="DV")
            ev_done[dob] = S.val("DV")
        slot_mm[slot] = S.val("PE")
        ev4 = S.val("DV")
        S.add("sync",
              (lambda ai=ai, sloto=sloto: lambda sy: sy.dma_start(
                  out=w2p[ai],
                  in_=foldo[:, sloto * 2048:(sloto + 1) * 2048]))(),
              waits=[("DV", ev4)], inc="sSt", inc_amt=D16)
        S.add("sync", lambda sy: sy.dma_start(out=fl_dram, in_=flsb[:]),
              inc="sSt", inc_amt=D16)
        st_done[sloto] = S.val("sSt")
    w2p_all_st = S.val("sSt")
    S.flush()
    ph01.close()

    if stop_after <= 1:
        emit_scratch_taps([("w2p", w2p, [NHC, 128, 2048], BF16),
                           ("u", u_dram, [128, 64, 2048], BF16)])
        S.add("sync", lambda sy: sy.nop(),
              waits=[("sSt", S.val("sSt")), ("sX", S.val("sX")),
                     ("gW", S.val("gW"))])
        S.flush()
        stack.close()
        return nc, tap_h

    ph23 = contextlib.ExitStack()
    ln1_fm = sb(ph23, "ln1_fm", [128, NDC * L], BF16)
    ln1_tm = sb(ph23, "ln1_tm", [128, D * 2], BF16)
    wq_g = sb(ph23, "wq_g", [128, NDC * 256], BF16)
    wk_g = sb(ph23, "wk_g", [128, NDC * 256], BF16)
    wv_g = sb(ph23, "wv_g", [128, NDC * 256], BF16)
    qk_sb = sb(ph23, "qk_sb", [128, 512 * 2], BF16)
    eq_t = sb(ph23, "eq_t", [128, 128 * 4], BF16)
    ek_strip = sb(ph23, "ek_strip", [128, 2 * L], BF16)
    fq_fm = sb(ph23, "fq_fm", [128, 4 * L], BF16)
    fk_fm = sb(ph23, "fk_fm", [128, 4 * L], BF16)
    v65 = sb(ph23, "v65", [128, NT * 260], BF16)
    rk = sb(ph23, "rk", [128, NT * 4], F32)
    ksum = sb(ph23, "ksum", [128, 4], F32)
    s_sb = sb(ph23, "s_sb", [128, 4 * SLEN], BF16)
    st_f32 = sb(ph23, "st_f32", [128, 130], F32)
    st_pad = sb(ph23, "st_pad", [128, 260], BF16)
    y_tm = sb(ph23, "y_tm", [128, 128 * 2], BF16)
    y_strip = sb(ph23, "y_strip", [128, 2 * 512], BF16)
    den_t = sb(ph23, "den_t", [128, 4], F32)
    rec_t = sb(ph23, "rec_t", [128, 4], F32)
    pA = psum(ph23, "pA")
    pB = psum(ph23, "pB")
    pC = psum(ph23, "pC")
    pD = psum(ph23, "pD")
    pE = psum(ph23, "pE")
    pF = psum(ph23, "pF")
    tpP = [psum(ph23, "tp0", BF16), psum(ph23, "tp1", BF16)]

    ln_sqrt_ac = [0, 0]

    def ln_stats(src_fn, first_waits, par=0):
        for j in range(4):
            S.add("vector",
                  (lambda j=j, par=par: lambda v: v.bn_stats(
                      lnst[:, 24 * par + 6 * j:24 * par + 6 * j + 6],
                      src_fn(j)))(),
                  waits=(list(first_waits) + [("AC", ln_sqrt_ac[par])])
                  if j == 0 else [], inc="DV")
        v4 = S.val("DV")
        S.add("vector",
              (lambda par=par: lambda v: v.bn_aggr(
                  lnmv[:, 2 * par:2 * par + 2],
                  lnst[:, 24 * par:24 * par + 24]))(),
              waits=[("DV", v4)], inc="DV")
        return S.val("DV")

    def ln_finish(par, va):
        S.add("scalar",
              (lambda par=par: lambda s: s.activation(
                  ln_s2[:, par:par + 1], lnmv[:, 2 * par + 1:2 * par + 2],
                  AF.Sqrt, bias=eps_sb[:], scale=1.0))(),
              waits=[("DV", va)], inc="AC")
        ln_sqrt_ac[par] = S.val("AC")
        vs = S.val("AC")
        S.add("vector",
              (lambda par=par: lambda v: v.reciprocal(
                  ln_s2[:, par:par + 1], ln_s2[:, par:par + 1]))(),
              waits=[("AC", vs)], inc="DV")
        vr = S.val("DV")
        S.add("vector",
              (lambda par=par: lambda v: v.tensor_scalar(
                  ln_s1[:, par:par + 1], lnmv[:, 2 * par:2 * par + 1], -1.0,
                  None, ALU.mult))(),
              waits=[("DV", vr)], inc="DV")
        return S.val("DV")

    def emit_ln_stats(src_fn, first_waits, par=0):
        va = ln_stats(src_fn, first_waits, par)
        return ln_finish(par, va)

    tp_free = [0, 0]
    apply_done = [0, 0]
    ln1tm_free = [0, 0]
    stage2 = {}
    for tc in range(NT + 1):
        if tc < NT:
            slot = tc % 2
            S.add("sync",
                  (lambda tc=tc, slot=slot: lambda sy: sy.dma_start(
                      out=x0_t[:, slot * D:(slot + 1) * D],
                      in_=x_in[tc * 128:(tc + 1) * 128, :]))(),
                  waits=[("DV", apply_done[slot])], inc="sX", inc_amt=D16)
            x_ld = flush("sync", "sX")
            va = ln_stats(
                (lambda slot=slot: lambda j:
                 x0_t[:, slot * D + 512 * j:slot * D + 512 * (j + 1)])(),
                [("sX", x_ld)], par=slot)
            stage2[tc] = va
        if tc >= 1:
            tp_ = tc - 1
            slot = tp_ % 2
            vstats = ln_finish(slot, stage2[tp_])
            S.add("vector",
                  (lambda slot=slot: lambda v: v.tensor_scalar(
                      ln1_tm[:, slot * D:(slot + 1) * D],
                      x0_t[:, slot * D:(slot + 1) * D],
                      ln_s1[:, slot:slot + 1], ln_s2[:, slot:slot + 1],
                      ALU.add, ALU.mult))(),
                  waits=[("DV", vstats), ("PE", ln1tm_free[slot])], inc="DV")
            apply_done[slot] = S.val("DV")
            for q4 in range(4):
                hp = q4 % 2
                for j in range(4):
                    dc = q4 * 4 + j
                    S.add("tensor",
                          (lambda slot=slot, dc=dc, hp=hp, j=j: lambda t: t.transpose(
                              tpP[hp][:, j * 128:(j + 1) * 128],
                              ln1_tm[:, slot * D + dc * 128:slot * D + (dc + 1) * 128],
                              ident[:]))(),
                          waits=([("DV", apply_done[slot]), ("DV", tp_free[hp])]
                                 if j == 0 else []),
                          inc="PE" if j == 3 else None)
                pe_tp = S.val("PE")
                S.add("vector",
                      (lambda tc2=tp_, q4=q4, hp=hp: lambda v: v.tensor_copy(
                          ln1_fm.ap().rearrange("p (dc t) -> p dc t", t=L)[
                              :, q4 * 4:(q4 + 1) * 4, tc2 * 128:(tc2 + 1) * 128],
                          tpP[hp][:, 0:512].rearrange(
                              "p (dc t) -> p dc t", t=128)))(),
                      waits=[("PE", pe_tp)], inc="DV")
                tp_free[hp] = S.val("DV")
            ln1tm_free[slot] = S.val("PE")
        ln1_done = S.val("DV")

    t_ln1 = tap("ln1_fm", [128, NDC * L], BF16)
    if t_ln1 is not None:
        S.add("sync", lambda sy: sy.dma_start(out=t_ln1, in_=ln1_fm[:]),
              waits=[("DV", ln1_done)], inc="sSt", inc_amt=D16)

    if stop_after <= 2:
        S.add("sync", lambda sy: sy.nop(),
              waits=[("sSt", S.val("sSt")), ("sX", S.val("sX")),
                     ("gW", S.val("gW")), ("DV", ln1_done)])
        S.flush()
        ph23.close()
        stack.close()
        return nc, tap_h

    S.add("vector", lambda v: v.memset(fq_fm[:], 0.0))
    S.add("vector", lambda v: v.memset(fk_fm[:], 0.0), inc="DV")
    strips_z = S.val("DV")
    prev_group_pe = 0
    prev_proj_pe = 0
    ystrip_st = [0, 0]
    for g in range(ngroups):
        wld = {}
        for (wbuf, wsrc) in ((wk_g, wkT), (wq_g, wqT), (wv_g, wvT)):
            for dc in range(NDC):
                S.add("gpsimd",
                      (lambda wbuf=wbuf, wsrc=wsrc, dc=dc, g=g: lambda gp:
                       gp.dma_start(
                           out=wbuf[:, dc * 256:(dc + 1) * 256],
                           in_=wsrc[dc * 128:(dc + 1) * 128,
                                    g * 256:(g + 1) * 256]))(),
                      waits=[("PE", prev_proj_pe)] if dc == 0 else [],
                      inc="gW", inc_amt=D16)
            wld[id(wbuf)] = flush("gpsimd", "gW")

        qkbank = [pA, pB]
        zbank = [pC, pD]
        qk_free = [0, 0]
        z_free = [0, 0]
        tsc_tp_pe = {}

        def proj_feat(which, wbuf, wfbd, dst_fm, g=g, wld=wld,
                      tsc_tp_pe=tsc_tp_pe, qk_free=qk_free, z_free=z_free):
            for p in range(2):
                for tsc in range(4):
                    u = p * 4 + tsc
                    slot = u % 2
                    bank = qkbank[slot]
                    for dc in range(NDC):
                        S.add("tensor",
                              (lambda bank=bank, wbuf=wbuf, dc=dc, p=p, tsc=tsc:
                               lambda t: t.matmul(
                                   bank[:, 0:512],
                                   wbuf[:, dc * 256 + p * 128:dc * 256 + (p + 1) * 128],
                                   ln1_fm[:, dc * L + tsc * 512:dc * L + (tsc + 1) * 512],
                                   start=(dc == 0), stop=(dc == NDC - 1)))(),
                              waits=([("gW", wld[id(wbuf)]), ("DV", ln1_done),
                                      ("AC", qk_free[slot])] if dc == 0 else []),
                              inc="PE" if dc == NDC - 1 else None)
                    pe_p = S.val("PE")
                    S.add("scalar",
                          (lambda bank=bank, slot=slot: lambda s: s.copy(
                              qk_sb[:, slot * 512:(slot + 1) * 512],
                              bank[:, 0:512]))(),
                          waits=[("PE", pe_p)], inc="AC")
                    ac_ev = S.val("AC")
                    qk_free[slot] = ac_ev
                    exp_acs = []
                    prev_tp = tsc_tp_pe.get((which, p, tsc - 1), 0)
                    if which == "q":
                        prev_tp = max(prev_tp, tsc_tp_pe.get(("q", p - 1, 3), 0))
                    else:
                        prev_tp = max(prev_tp, tsc_tp_pe.get(("k", p - 1, 3), 0))
                    for tci in range(4):
                        tc = tsc * 4 + tci
                        zb = zbank[tci % 2]
                        S.add("tensor",
                              (lambda zb=zb, slot=slot, tci=tci, wfbd=wfbd,
                                      g=g, p=p: lambda t: t.matmul(
                                  zb[:, 0:128],
                                  qk_sb[:, slot * 512 + tci * 128:
                                        slot * 512 + (tci + 1) * 128],
                                  wfbd[:, (2 * g + p) * 128:(2 * g + p + 1) * 128],
                                  start=True, stop=True))(),
                              waits=[("AC", ac_ev), ("gW", g_wf),
                                     ("AC", z_free[tci % 2])],
                              inc="PE")
                        pe_z = S.val("PE")
                        if which == "k":
                            dst = (lambda p=p, tc=tc:
                                   ek_strip[:, p * L + tc * 128:p * L + (tc + 1) * 128])
                        else:
                            dst = (lambda tci=tci:
                                   eq_t[:, tci * 128:(tci + 1) * 128])
                        S.add("scalar",
                              (lambda dst=dst, zb=zb: lambda s: s.activation(
                                  dst(), zb[:, 0:128], AF.Exp, bias=0.0,
                                  scale=1.0))(),
                              waits=[("PE", pe_z)] +
                                    ([("PE", prev_tp)] if (which == "q" and tci == 0)
                                     else []),
                              inc="AC")
                        z_free[tci % 2] = S.val("AC")
                        exp_acs.append(S.val("AC"))
                        if which == "k":
                            S.add("vector",
                                  (lambda p=p, tc=tc: lambda v: v.reduce_sum(
                                      ksum[:, 0:2],
                                      ek_strip[:, p * L + tc * 128:
                                               p * L + (tc + 1) * 128]
                                      .rearrange("p (hh f) -> p hh f", hh=2),
                                      axis=mybir.AxisListType.X))(),
                                  waits=[("AC", S.val("AC"))], inc="DV")
                            vks = S.val("DV")
                            S.add("vector",
                                  (lambda p=p, tc=tc: lambda v: v.reciprocal(
                                      rk[:, tc * 4 + 2 * p:tc * 4 + 2 * p + 2],
                                      ksum[:, 0:2]))(),
                                  waits=[("DV", vks)], inc="DV")
                    tph = u % 2
                    for tci in range(4):
                        tc = tsc * 4 + tci
                        if which == "k":
                            src = (lambda p=p, tc=tc:
                                   ek_strip[:, p * L + tc * 128:p * L + (tc + 1) * 128])
                        else:
                            src = (lambda tci=tci:
                                   eq_t[:, tci * 128:(tci + 1) * 128])
                        S.add("tensor",
                              (lambda src=src, tph=tph, tci=tci: lambda t:
                               t.transpose(
                                   tpP[tph][:, tci * 128:(tci + 1) * 128],
                                   src(), ident[:]))(),
                              waits=([("AC", exp_acs[3]), ("DV", tp_free[tph])]
                                     if tci == 0 else []),
                              inc="PE" if tci == 3 else None)
                    pe_tp4 = S.val("PE")
                    tsc_tp_pe[(which, p, tsc)] = pe_tp4
                    for par in range(2):
                        hstrip = 2 * p + par
                        S.add("vector",
                              (lambda dst_fm=dst_fm, hstrip=hstrip, par=par,
                                      tsc=tsc, tph=tph: lambda v: v.tensor_copy(
                                  dst_fm[64 * par:64 * par + 64,
                                         hstrip * L + tsc * 512:
                                         hstrip * L + (tsc + 1) * 512],
                                  tpP[tph][64 * par:64 * par + 64, 0:512]))(),
                              waits=[("PE", pe_tp4), ("DV", strips_z)]
                              if par == 0 else [],
                              inc="DV" if par == 1 else None)
                    tp_free[tph] = S.val("DV")
            return

        proj_feat("k", wk_g, wfk_bd, fk_fm)
        k_done = S.val("DV")
        if attn_sub == "k":
            break
        proj_feat("q", wq_g, wfq_bd, fq_fm)
        q_done = S.val("DV")
        q_ac = S.val("AC")
        if attn_sub == "kq":
            break

        v_free = [0, 0]
        for tc in range(NT):
            slot = tc % 2
            bank = qkbank[slot]
            for dc in range(NDC):
                S.add("tensor",
                      (lambda bank=bank, dc=dc, tc=tc, g=g: lambda t: t.matmul(
                          bank[:, 0:256],
                          ln1_fm[:, dc * L + tc * 128:dc * L + (tc + 1) * 128],
                          wv_g[:, dc * 256:(dc + 1) * 256],
                          start=(dc == 0), stop=(dc == NDC - 1)))(),
                      waits=([("gW", wld[id(wv_g)]), ("DV", k_done),
                              ("AC", q_ac),
                              ("DV", v_free[slot])] if dc == 0 else []),
                      inc="PE" if dc == NDC - 1 else None)
            pe_v = S.val("PE")
            for hh in range(4):
                S.add("vector",
                      (lambda bank=bank, tc=tc, hh=hh: lambda v: v.tensor_scalar(
                          v65[:, tc * 260 + hh * 65:tc * 260 + hh * 65 + 64],
                          bank[:, hh * 64:(hh + 1) * 64],
                          rk[:, tc * 4 + hh:tc * 4 + hh + 1], None, ALU.mult))(),
                      waits=[("PE", pe_v)] if hh == 0 else [])
            S.add("vector",
                  (lambda tc=tc: lambda v: v.tensor_copy(
                      v65.ap().rearrange("p (t hh c) -> p t hh c", t=NT, hh=4)[
                          :, tc, :, 64:65].rearrange("p hh c -> p (hh c)"),
                      rk[:, tc * 4:(tc + 1) * 4]))(),
                  inc="DV")
            v_free[slot] = S.val("DV")
        v_done = S.val("DV")
        prev_proj_pe = S.val("PE")
        if attn_sub == "kqv":
            break

        S.add("vector", lambda v: v.memset(st_f32[:], 0.0),
              waits=[("PE", prev_group_pe)])
        S.add("vector", lambda v: v.memset(st_pad[:], 0.0), inc="DV")
        st_ready = S.val("DV")

        sbank = [pA, pB]
        ybank = [pC, pD]
        stbank = pE
        s_ev = {}
        prev_y_pe = 0
        ytm_free_pe = 0
        ybank_free_dv = 0
        for sc in range(NSC):
            sb_free = s_ev.get("free", [(0, 0), (0, 0)])
            it_s = 0
            for hh in range(4):
                p, par = hh // 2, hh % 2
                for j in range(4):
                    bslot_s = it_s % 2
                    bank = sbank[bslot_s]
                    tk = sc * 4 + j
                    nj = 512 - j * 128
                    w0 = [("AC", sb_free[bslot_s][0]), ("DV", sb_free[bslot_s][1])]
                    if it_s == 0:
                        w0 += [("DV", v_done), ("PE", prev_y_pe)]
                    S.add("tensor",
                          (lambda bank=bank, hh=hh, tk=tk, j=j, sc=sc,
                                  nj=nj: lambda t: t.matmul(
                              bank[:, 0:nj],
                              fk_fm[:, hh * L + tk * 128:hh * L + (tk + 1) * 128],
                              fq_fm[:, hh * L + sc * 512 + j * 128:
                                    hh * L + (sc + 1) * 512],
                              start=True, stop=True))(),
                          waits=w0, inc="PE")
                    pe_s = S.val("PE")
                    base = hh * SLEN + SOFF[j]
                    S.add("vector",
                          (lambda bank=bank, base=base: lambda v: v.tensor_tensor(
                              s_sb[:, base:base + 128], bank[:, 0:128],
                              mask_tri[:], ALU.mult))(),
                          waits=[("PE", pe_s)],
                          inc=None if nj > 128 else "DV")
                    if nj > 128:
                        S.add("vector",
                              (lambda bank=bank, base=base, nj=nj: lambda v:
                               v.tensor_copy(s_sb[:, base + 128:base + nj],
                                             bank[:, 128:nj]))(),
                              inc="DV")
                    sb_free[bslot_s] = (S.val("AC"), S.val("DV"))
                    it_s += 1
            s_ev["free"] = sb_free
            s_ac, s_dv = S.val("AC"), S.val("DV")
            if attn_sub == "s":
                prev_y_pe = S.val("PE")
                continue

            for i in range(4):
                tq = sc * 4 + i
                for hh in range(4):
                    p, par = hh // 2, hh % 2
                    bank = ybank[par]
                    S.add("tensor",
                          (lambda bank=bank, p=p, hh=hh, tq=tq, st=(hh < 2):
                           lambda t: t.matmul(
                               bank[:, 65 * p:65 * p + 65],
                               fq_fm[:, hh * L + tq * 128:hh * L + (tq + 1) * 128],
                               st_pad[:, hh * 65:hh * 65 + 65],
                               start=st, stop=False))(),
                          waits=([("DV", st_ready), ("AC", s_ac), ("DV", s_dv),
                                  ("DV", ybank_free_dv)] if hh == 0 else []))
                for hh in range(4):
                    p, par = hh // 2, hh % 2
                    bank = ybank[par]
                    for j in range(i + 1):
                        off = hh * SLEN + SOFF[j] + (i - j) * 128
                        last = (j == i) and (hh >= 2)
                        S.add("tensor",
                              (lambda bank=bank, p=p, off=off, hh=hh, sc=sc,
                                      j=j, last=last: lambda t: t.matmul(
                                  bank[:, 65 * p:65 * p + 65],
                                  s_sb[:, off:off + 128],
                                  v65[:, (sc * 4 + j) * 260 + hh * 65:
                                      (sc * 4 + j) * 260 + hh * 65 + 65],
                                  start=False, stop=last))(),
                              inc="PE" if (j == i and hh == 3) else None)
                pe_y = S.val("PE")
                prev_y_pe = pe_y
                S.add("vector",
                      lambda v: v.tensor_scalar(den_t[:, 0:2], pC[:, 64:130:65],
                                                DEN_EPS, None, ALU.add),
                      waits=[("PE", pe_y)])
                S.add("vector",
                      lambda v: v.tensor_scalar(den_t[:, 2:4], pD[:, 64:130:65],
                                                DEN_EPS, None, ALU.add),
                      inc="DV")
                v_den = S.val("DV")
                S.add("vector", lambda v: v.reciprocal(rec_t[:], den_t[:]),
                      waits=[("DV", v_den)], inc="DV")
                v_rec = S.val("DV")
                for hh in range(4):
                    p, par = hh // 2, hh % 2
                    bank = ybank[par]
                    S.add("vector",
                          (lambda bank=bank, p=p, par=par, ridx=2 * par + p:
                           lambda v: v.tensor_scalar(
                               y_tm[:, p * 128 + par * 64:p * 128 + par * 64 + 64],
                               bank[:, 65 * p:65 * p + 64],
                               rec_t[:, ridx:ridx + 1], None, ALU.mult))(),
                          waits=([("DV", v_rec), ("PE", ytm_free_pe)]
                                 if hh == 0 else []),
                          inc="DV" if hh == 3 else None)
                v_ytm = S.val("DV")
                ybank_free_dv = v_ytm
                tph = i % 2
                for p in range(2):
                    S.add("tensor",
                          (lambda p=p, tph=tph, i=i: lambda t: t.transpose(
                              tpP[tph][:, ((2 * i) % 4 + p) * 128:
                                       ((2 * i) % 4 + p + 1) * 128],
                              y_tm[:, p * 128:(p + 1) * 128], ident[:]))(),
                          waits=([("DV", v_ytm), ("DV", tp_free[tph])]
                                 if p == 0 else []),
                          inc="PE" if p == 1 else None)
                pe_ytp = S.val("PE")
                ytm_free_pe = pe_ytp
                for p in range(2):
                    wv = [("PE", pe_ytp)]
                    if i == 0:
                        wv.append(("sSt", ystrip_st[p]))
                    S.add("vector",
                          (lambda p=p, tph=tph, i=i: lambda v: v.tensor_copy(
                              y_strip[:, p * 512 + i * 128:p * 512 + (i + 1) * 128],
                              tpP[tph][:, ((2 * i) % 4 + p) * 128:
                                       ((2 * i) % 4 + p + 1) * 128]))(),
                          waits=wv, inc="DV" if p == 1 else None)
                tp_free[tph] = S.val("DV")
            y_ev = S.val("DV")
            if attn_sub == "sy":
                for p in range(2):
                    for tci4 in range(4):
                        S.add("sync",
                              (lambda g=g, p=p, sc=sc, tci4=tci4: lambda sy:
                               sy.dma_start(
                                   out=y_dram[2 * g + p, sc * 4 + tci4],
                                   in_=y_strip[:, p * 512 + tci4 * 128:
                                               p * 512 + (tci4 + 1) * 128]))(),
                              waits=[("DV", y_ev)] if tci4 == 0 else [],
                              inc="sSt", inc_amt=D16)
                    ystrip_st[p] = S.val("sSt")
                continue

            first = True
            for hh in range(4):
                p, par = hh // 2, hh % 2
                for j in range(4):
                    tk = sc * 4 + j
                    last = (hh == 3 and j == 3)
                    S.add("tensor",
                          (lambda p=p, par=par, tk=tk, hh=hh, st0=(j == 0),
                                  last=last: lambda t: t.matmul(
                              stbank[64 * par:64 * par + 64, 65 * p:65 * p + 65],
                              ek_strip[:, p * L + tk * 128 + 64 * par:
                                       p * L + tk * 128 + 64 * par + 64],
                              v65[:, tk * 260 + hh * 65:tk * 260 + (hh + 1) * 65],
                              start=st0, stop=last, skip_group_check=True))(),
                          waits=[("DV", st_ready)] if first else [],
                          inc="PE" if last else None)
                    first = False
            pe_st = S.val("PE")
            S.add("vector",
                  lambda v: v.tensor_add(st_f32[:, 0:130], st_f32[:, 0:130],
                                         stbank[:, 0:130]),
                  waits=[("PE", pe_st)], inc="DV")
            v_sta = S.val("DV")
            for hh in range(4):
                p, par = hh // 2, hh % 2
                S.add("vector",
                      (lambda p=p, par=par, hh=hh: lambda v: v.tensor_copy(
                          st_pad[64 * par:64 * par + 64, hh * 65:hh * 65 + 65],
                          st_f32[64 * par:64 * par + 64, 65 * p:65 * p + 65]))(),
                      waits=[("DV", v_sta)] if hh == 0 else [],
                      inc="DV" if hh == 3 else None)
            st_ready = S.val("DV")

            for p in range(2):
                for tci4 in range(4):
                    S.add("sync",
                          (lambda g=g, p=p, sc=sc, tci4=tci4: lambda sy:
                           sy.dma_start(
                               out=y_dram[2 * g + p, sc * 4 + tci4],
                               in_=y_strip[:, p * 512 + tci4 * 128:
                                           p * 512 + (tci4 + 1) * 128]))(),
                          waits=[("DV", y_ev)] if tci4 == 0 else [],
                          inc="sSt", inc_amt=D16)
                ystrip_st[p] = S.val("sSt")
        prev_group_pe = S.val("PE")
    y_all_st = S.val("sSt")
    for nm, buf in (("fkfm", fk_fm), ("fqfm", fq_fm), ("ek", ek_strip),
                    ("v65t", v65), ("rkt", rk)):
        t = tap(nm, [128, buf.shape[1]], BF16 if nm != "rkt" else F32)
        if t is not None:
            S.add("sync",
                  (lambda t=t, buf=buf: lambda sy: sy.dma_start(out=t, in_=buf[:]))(),
                  waits=[("DV", S.val("DV")), ("AC", S.val("AC")),
                         ("PE", S.val("PE"))],
                  inc="sSt", inc_amt=D16)
    S.flush()
    ph23.close()

    if stop_after <= 3:
        emit_scratch_taps([("w2p", w2p, [NHC, 128, 2048], BF16), ("y", y_dram, [NDC, NT, 128, 128], BF16)])
        S.add("sync", lambda sy: sy.nop(),
              waits=[("sSt", S.val("sSt")), ("sX", S.val("sX")),
                     ("gW", S.val("gW"))])
        S.flush()
        stack.close()
        return nc, tap_h

    ph4 = contextlib.ExitStack()
    woT_sb = sb(ph4, "woT_sb", [128, NDC * 2048], BF16)
    y_t = sb(ph4, "y_t", [128, 2048 * 2], BF16)
    ao_sb = sb(ph4, "ao_sb", [128, D], F32)
    x1_sb = sb(ph4, "x1_sb", [128, D], F32)
    ln2_tm = sb(ph4, "ln2_tm", [128, D], BF16)
    stg = sb(ph4, "stg", [128, 512 * 2], BF16)
    pW = [psum(ph4, f"pw{i}") for i in range(4)]
    tpP = [psum(ph4, "tp0b", BF16), psum(ph4, "tp1b", BF16)]
    tp_free = [0, 0]

    for dc in range(NDC):
        for dob in range(4):
            S.add("gpsimd",
                  (lambda dc=dc, dob=dob: lambda gp: gp.dma_start(
                      out=woT_sb[:, dc * 2048 + dob * 512:dc * 2048 + (dob + 1) * 512],
                      in_=woT[dc * 128:(dc + 1) * 128, dob * 512:(dob + 1) * 512]))(),
                  inc="gW", inc_amt=D16)
    g_wo = flush("gpsimd", "gW")

    y_used = [0, 0]
    x0_used = [0, 0]
    bank_free = [0] * 4
    stg_free = [0, 0]
    ao_free = 0
    x1st_prev = 0
    ln2tm_free = 0
    for tc in range(NT):
        slot = tc % 2
        for k in range(NDC):
            S.add("sync",
                  (lambda tc=tc, slot=slot, k=k: lambda sy: sy.dma_start(
                      out=y_t[:, slot * 2048 + k * 128:slot * 2048 + (k + 1) * 128],
                      in_=y_dram[k, tc]))(),
                  waits=([("sSt", y_all_st), ("PE", y_used[slot])]
                         if k == 0 else []),
                  inc="sX", inc_amt=D16)
        y_ld = flush("sync", "sX")
        S.add("sync",
              (lambda tc=tc, slot=slot: lambda sy: sy.dma_start(
                  out=x0_t[:, slot * D:(slot + 1) * D],
                  in_=x_in[tc * 128:(tc + 1) * 128, :]))(),
              waits=[("DV", x0_used[slot])], inc="sX", inc_amt=D16)
        x0_ld = flush("sync", "sX")
        for dob in range(4):
            bank = pW[dob]
            for k in range(NDC):
                S.add("tensor",
                      (lambda bank=bank, k=k, dob=dob, slot=slot: lambda t:
                       t.matmul(
                           bank[:, 0:512],
                           y_t[:, slot * 2048 + k * 128:slot * 2048 + (k + 1) * 128],
                           woT_sb[:, k * 2048 + dob * 512:k * 2048 + (dob + 1) * 512],
                           start=(k == 0), stop=(k == NDC - 1)))(),
                      waits=([("sX", y_ld), ("gW", g_wo), ("AC", bank_free[dob])]
                             if k == 0 else []),
                      inc="PE" if k == NDC - 1 else None)
            pe_wo = S.val("PE")
            S.add("scalar",
                  (lambda bank=bank, dob=dob: lambda s: s.copy(
                      ao_sb[:, dob * 512:(dob + 1) * 512], bank[:, 0:512]))(),
                  waits=[("PE", pe_wo)] + ([("DV", ao_free)] if dob == 0 else []),
                  inc="AC")
            bank_free[dob] = S.val("AC")
        y_used[slot] = S.val("PE")
        ac_ao = S.val("AC")
        vst = emit_ln_stats(lambda j: ao_sb[:, 512 * j:512 * (j + 1)],
                            [("AC", ac_ao)])
        S.add("vector",
              lambda v: v.tensor_scalar(ao_sb[:], ao_sb[:], ln_s1[:, 0:1], ln_s2[:, 0:1],
                                        ALU.add, ALU.mult),
              waits=[("DV", vst)])
        S.add("vector",
              (lambda slot=slot: lambda v: v.tensor_add(
                  x1_sb[:], ao_sb[:], x0_t[:, slot * D:(slot + 1) * D]))(),
              waits=[("sX", x0_ld), ("sSt", x1st_prev)], inc="DV")
        x0_used[slot] = S.val("DV")
        v_x1 = S.val("DV")
        ao_free = v_x1
        S.add("sync",
              (lambda tc=tc: lambda sy: sy.dma_start(
                  out=x1_dram[tc * 128:(tc + 1) * 128, :], in_=x1_sb[:]))(),
              waits=[("DV", v_x1)], inc="sSt", inc_amt=D16)
        x1st_prev = S.val("sSt")
        vst2 = emit_ln_stats(lambda j: x1_sb[:, 512 * j:512 * (j + 1)], [])
        S.add("vector",
              lambda v: v.tensor_scalar(ln2_tm[:], x1_sb[:], ln_s1[:, 0:1], ln_s2[:, 0:1],
                                        ALU.add, ALU.mult),
              waits=[("DV", vst2), ("PE", ln2tm_free)], inc="DV")
        v_ln2 = S.val("DV")
        for q4 in range(4):
            hp = q4 % 2
            for j in range(4):
                dc = q4 * 4 + j
                S.add("tensor",
                      (lambda dc=dc, hp=hp, j=j: lambda t: t.transpose(
                          tpP[hp][:, j * 128:(j + 1) * 128],
                          ln2_tm[:, dc * 128:(dc + 1) * 128], ident[:]))(),
                      waits=([("DV", v_ln2), ("DV", tp_free[hp])]
                             if j == 0 else []),
                      inc="PE" if j == 3 else None)
            pe_tp = S.val("PE")
            S.add("vector",
                  (lambda hp=hp: lambda v: v.tensor_copy(
                      stg[:, hp * 512:(hp + 1) * 512], tpP[hp][:, 0:512]))(),
                  waits=[("PE", pe_tp), ("sSt", stg_free[hp])], inc="DV")
            tp_free[hp] = S.val("DV")
            v_stg = S.val("DV")
            blk_i, ci = tc // 4, tc % 4
            for j4 in range(4):
                S.add("sync",
                      (lambda blk_i=blk_i, ci=ci, q4=q4, hp=hp, j4=j4:
                       lambda sy: sy.dma_start(
                           out=ln2_dram[blk_i, q4 * 4 + j4, :,
                                        ci * 128:(ci + 1) * 128],
                           in_=stg[:, hp * 512 + j4 * 128:
                                   hp * 512 + (j4 + 1) * 128]))(),
                      waits=[("DV", v_stg)] if j4 == 0 else [],
                      inc="sSt", inc_amt=D16)
            stg_free[hp] = S.val("sSt")
        ln2tm_free = S.val("PE")
    ln2_all_st = S.val("sSt")
    x1_all_st = ln2_all_st
    S.flush()
    ph4.close()

    if stop_after <= 4:
        emit_scratch_taps([("w2p", w2p, [NHC, 128, 2048], BF16), ("y", y_dram, [NDC, NT, 128, 128], BF16), ("x1", x1_dram, [L, D], F32), ("ln2", ln2_dram, [NBLK, NDC, 128, 512], BF16)])
        S.add("sync", lambda sy: sy.nop(),
              waits=[("sSt", S.val("sSt")), ("sX", S.val("sX")),
                     ("gW", S.val("gW"))])
        S.flush()
        stack.close()
        return nc, tap_h

    ph5 = contextlib.ExitStack()
    ln2_blk = sb(ph5, "ln2_blk", [128, NDC * 512], BF16)
    h_fm = sb(ph5, "h_fm", [128, NHC * 512], BF16)
    w13_sb = sb(ph5, "w13_sb", [128, 4096 * 3], BF16)
    w2p_sb = sb(ph5, "w2p_sb", [128, 1024 * 6], BF16)
    silu_sb = sb(ph5, "silu_sb", [128, 512 * 2], F32)
    fo_sb = sb(ph5, "fo_sb", [128, 4 * D], F32)
    x1_t = sb(ph5, "x1_t", [128, D * 2], F32)
    wecho = None
    pU = [psum(ph5, f"pu{i}") for i in range(8)]

    ln2_ld = 0
    ln2_used = 0
    w13_ld = [0, 0, 0]
    w13_used = [0, 0, 0]
    w2p_fl = {}
    w2p_used6 = [0] * 6
    w2p_emitted = set()
    silu_free = [0, 0]
    h_used = 0
    ac_fo_prev = 0
    out_st = 0
    x1_used = [0, 0]
    fo_free = [0] * 8
    last_out = 0
    def emit_w2p_load(j, blk_i):
        if j in w2p_emitted or j >= 128:
            return
        w2p_emitted.add(j)
        ws = j % 6
        phj, kj = divmod(j, 64)
        S.add("gpsimd",
              (lambda kj=kj, ws=ws, phj=phj: lambda e:
               e.dma_start(
                   out=w2p_sb[:, ws * 1024:(ws + 1) * 1024],
                   in_=w2p[kj][:, phj * 1024:(phj + 1) * 1024]))(),
              waits=[("PE", w2p_used6[ws])] +
                    ([("sSt", w2p_all_st)]
                     if (blk_i == 0 and j == 0) else []),
              inc="gW", inc_amt=D16)
        if j % 2 == 1 or j == 127:
            fv = flush("gpsimd", "gW")
            w2p_fl[j] = ("gW", fv)
            if j - 1 not in w2p_fl or w2p_fl.get(j - 1) is None:
                w2p_fl[j - 1] = ("gW", fv)
        else:
            w2p_fl[j] = None

    for blk_i in range(NBLK):
        w2p_emitted.clear()
        for dc in range(NDC):
            S.add("sync",
                  (lambda blk_i=blk_i, dc=dc: lambda sy:
                   sy.dma_start(
                       out=ln2_blk[:, dc * 512:(dc + 1) * 512],
                       in_=ln2_dram[blk_i, dc]))(),
                  waits=([("sSt", ln2_all_st), ("PE", ln2_used)]
                         if dc == 0 else []),
                  inc="sX", inc_amt=D16)
        ln2_ld = flush("sync", "sX")

        for hc in range(NHC):
            ws = hc % 3
            for wi, wsrc in ((0, w1Tt), (1, w3Tt)):
                S.add("gpsimd",
                      (lambda hc=hc, ws=ws, wi=wi, wsrc=wsrc: lambda gp:
                       gp.dma_start(
                           out=w13_sb[:, ws * 4096 + wi * 2048:
                                      ws * 4096 + wi * 2048 + 2048],
                           in_=wsrc[hc].rearrange("p dc f -> p (dc f)")))(),
                      waits=[("PE", w13_used[ws])] if wi == 0 else [],
                      inc="gW", inc_amt=D16)
            w13_ld[ws] = flush("gpsimd", "gW")
            gb = pU[2 * (hc % 2)]
            ub = pU[2 * (hc % 2) + 1]
            for wi in range(2):
                bank = gb if wi == 0 else ub
                for dc in range(NDC):
                    S.add("tensor",
                          (lambda bank=bank, ws=ws, wi=wi, dc=dc:
                           lambda t: t.matmul(
                               bank[:, 0:512],
                               w13_sb[:, ws * 4096 + wi * 2048 + dc * 128:
                                      ws * 4096 + wi * 2048 + (dc + 1) * 128],
                               ln2_blk[:, dc * 512:(dc + 1) * 512],
                               start=(dc == 0), stop=(dc == NDC - 1)))(),
                          waits=([("gW", w13_ld[ws]), ("sX", ln2_ld),
                                  ("DV", silu_free[hc % 2]),
                                  ("AC", ac_fo_prev),
                                  ("PE", h_used)] if (dc == 0 and wi == 0)
                                 else []),
                          inc="PE" if dc == NDC - 1 else None)
            w13_used[ws] = S.val("PE")
            pe_up = S.val("PE")
            sslot = hc % 2
            S.add("scalar",
                  (lambda gb=gb, sslot=sslot: lambda s: s.activation(
                      silu_sb[:, sslot * 512:(sslot + 1) * 512], gb[:, 0:512],
                      AF.Silu, bias=0.0, scale=1.0))(),
                  waits=[("PE", pe_up), ("DV", silu_free[sslot])], inc="AC")
            ac_si = S.val("AC")
            S.add("vector",
                  (lambda ub=ub, sslot=sslot, hc=hc: lambda v: v.tensor_tensor(
                      h_fm[:, hc * 512:(hc + 1) * 512],
                      silu_sb[:, sslot * 512:(sslot + 1) * 512],
                      ub[:, 0:512], ALU.mult))(),
                  waits=[("AC", ac_si)], inc="DV")
            silu_free[hc % 2] = S.val("DV")
        h_ready = S.val("DV")
        ln2_used = S.val("PE")

        for ph in range(2):
            for k in range(NHC):
                kk = ph * 64 + k
                for j in range(kk, kk + 6):
                    emit_w2p_load(j, blk_i)
                ws = kk % 6
                for tci in range(4):
                    for dd in range(2):
                        bank = pU[tci * 2 + dd]
                        w_mm = []
                        if tci == 0 and dd == 0:
                            jt = min(kk + 4, 127)
                            if w2p_fl[jt] is None:
                                jt = min(jt + 1, 127)
                            w_mm.append(w2p_fl[jt])
                            if kk == 0:
                                w_mm.append(("DV", h_ready))
                        if k == 0:
                            w_mm.append(("AC", fo_free[tci * 2 + dd]))
                        S.add("tensor",
                              (lambda bank=bank, k=k, ws=ws, tci=tci, dd=dd,
                                      ph=ph: lambda t: t.matmul(
                                  bank[:, 0:512],
                                  h_fm[:, k * 512 + tci * 128:k * 512 + (tci + 1) * 128],
                                  w2p_sb[:, ws * 1024 + dd * 512:
                                         ws * 1024 + (dd + 1) * 512],
                                  start=(k == 0), stop=(k == NHC - 1)))(),
                              waits=w_mm,
                              inc="PE" if (tci == 3 and dd == 1) else None)
                w2p_used6[ws] = S.val("PE")
            pe_dn = S.val("PE")
            for tci in range(4):
                for dd in range(2):
                    bank = pU[tci * 2 + dd]
                    dob = 2 * ph + dd
                    w_ev = [("PE", pe_dn)]
                    if ph == 0 and tci == 0 and dd == 0 and blk_i > 0:
                        w_ev += [("DV", x1_used[0]), ("DV", x1_used[1]),
                                 ("sSt", last_out)]
                    S.add("scalar",
                          (lambda bank=bank, tci=tci, dob=dob: lambda s: s.copy(
                              fo_sb[:, tci * D + dob * 512:tci * D + (dob + 1) * 512],
                              bank[:, 0:512]))(),
                          waits=w_ev, inc="AC")
                    fo_free[tci * 2 + dd] = S.val("AC")
        ac_fo = S.val("AC")
        ac_fo_prev = ac_fo
        h_used = S.val("PE")
        if "fo" in taps and blk_i == NBLK - 1:
            t_fo = tap("fo", [128, 4 * D])
            S.add("sync",
                  (lambda t_fo=t_fo: lambda sy: sy.dma_start(
                      out=t_fo, in_=fo_sb[:]))(),
                  waits=[("AC", ac_fo)], inc="sSt", inc_amt=D16)
            fo_tap_st = S.val("sSt")
        else:
            fo_tap_st = 0

        for tci in range(4):
            tc = blk_i * 4 + tci
            xslot = tci % 2
            S.add("sync",
                  (lambda tc=tc, xslot=xslot: lambda sy: sy.dma_start(
                      out=x1_t[:, xslot * D:(xslot + 1) * D],
                      in_=x1_dram[tc * 128:(tc + 1) * 128, :]))(),
                  waits=[("sSt", x1_all_st), ("DV", x1_used[xslot])],
                  inc="sX", inc_amt=D16)
            x1_ld = flush("sync", "sX")
            vst = emit_ln_stats(
                (lambda tci=tci: lambda j:
                 fo_sb[:, tci * D + 512 * j:tci * D + 512 * (j + 1)])(),
                [("AC", ac_fo), ("sSt", fo_tap_st)])
            S.add("vector",
                  (lambda tci=tci: lambda v: v.tensor_scalar(
                      fo_sb[:, tci * D:(tci + 1) * D],
                      fo_sb[:, tci * D:(tci + 1) * D],
                      ln_s1[:, 0:1], ln_s2[:, 0:1], ALU.add, ALU.mult))(),
                  waits=[("DV", vst)])
            S.add("vector",
                  (lambda tci=tci, xslot=xslot: lambda v: v.tensor_add(
                      fo_sb[:, tci * D:(tci + 1) * D],
                      fo_sb[:, tci * D:(tci + 1) * D],
                      x1_t[:, xslot * D:(xslot + 1) * D]))(),
                  waits=[("sX", x1_ld)], inc="DV")
            x1_used[xslot] = S.val("DV")
            v_out = S.val("DV")
            S.add("sync",
                  (lambda tc=tc, tci=tci: lambda sy: sy.dma_start(
                      out=out[tc * 128:(tc + 1) * 128, :],
                      in_=fo_sb[:, tci * D:(tci + 1) * D]))(),
                  waits=[("DV", v_out)], inc="sSt", inc_amt=D16)
            last_out = S.val("sSt")

    t_h = tap("hfm", [128, NHC * 512], BF16)
    if t_h is not None:
        S.add("sync",
              lambda sy: sy.dma_start(out=t_h, in_=h_fm[:]),
              waits=[("DV", S.val("DV")), ("AC", S.val("AC")),
                     ("PE", S.val("PE"))], inc="sSt", inc_amt=D16)
    emit_scratch_taps([("w2p", w2p, [NHC, 128, 2048], BF16),
                       ("y", y_dram, [NDC, NT, 128, 128], BF16),
                       ("x1", x1_dram, [L, D], F32),
                       ("ln2", ln2_dram, [NBLK, NDC, 128, 512], BF16)])

    S.add("sync", lambda sy: sy.nop(), waits=[("sSt", S.val("sSt"))])
    S.flush()
    ph5.close()
    stack.close()
    return nc, tap_h


_PROG_CACHE = {}


def _get_prog(taps=()):
    key = tuple(sorted(taps))
    if key not in _PROG_CACHE:
        _PROG_CACHE[key] = build_program(taps=key)
    return _PROG_CACHE[key]


def prep_inputs(inputs):
    x = np.ascontiguousarray(np.asarray(inputs["L0_attn_hidden_states"]))
    wq = np.asarray(inputs["L0_attn_Wq"])
    wk = np.asarray(inputs["L0_attn_Wk"])
    wv = np.asarray(inputs["L0_attn_Wv"])
    wo = np.asarray(inputs["L0_attn_Wo"])
    wfq = np.ascontiguousarray(np.asarray(inputs["L0_attn_Wfq"]))
    wfk = np.ascontiguousarray(np.asarray(inputs["L0_attn_Wfk"]))
    w1 = np.asarray(inputs["L0_ffn_W1"])
    w2 = np.asarray(inputs["L0_ffn_W2"])
    w3 = np.asarray(inputs["L0_ffn_W3"])

    def tile13(w):
        wt = w.T
        return np.ascontiguousarray(
            wt.reshape(NDC, 128, NHC, 128).transpose(2, 1, 0, 3))

    common = {
        "wqT": np.ascontiguousarray(wq.T),
        "wkT": np.ascontiguousarray(wk.T),
        "wvT": np.ascontiguousarray(wv.T),
        "woT": np.ascontiguousarray(wo.T),
        "wfq": wfq, "wfk": wfk,
        "w1Tt": tile13(w1),
        "w3Tt": tile13(w3),
        "w2tp": np.ascontiguousarray(
            w2.T.reshape(64, 128, D).transpose(1, 0, 2).reshape(FF, D)),
    }
    common["pad_in"] = np.zeros((4096, 1024), np.float32)
    in_maps = []
    for b in range(B):
        m = dict(common)
        m["x_in"] = np.ascontiguousarray(x[b])
        in_maps.append(m)
    return in_maps


def kernel(**inputs):
    nc, _ = _get_prog()
    in_maps = prep_inputs(inputs)
    res = run_bass_kernel_spmd(nc, in_maps, core_ids=list(range(B)))
    return np.stack([np.asarray(res.results[b]["out"]) for b in range(B)], axis=0)


# version 08
# speedup vs baseline: 1.4721x; 1.4721x over previous
import contextlib
import math

import numpy as np

import concourse.bass as bass
import concourse.mybir as mybir
from concourse.bass_utils import run_bass_kernel_spmd

F32 = mybir.dt.float32
BF16 = mybir.dt.bfloat16
I32 = mybir.dt.int32
AF = mybir.ActivationFunctionType
ALU = mybir.AluOpType

B = 8
L = 2048
D = 2048
H = 32
FD = 64
FF = 8192
NT = L // 128
NDC = D // 128
NSC = 4
NG = 8
NBLK = 4
NHC = FF // 128
LN_EPS = 1e-5
DEN_EPS = 1e-12
SOFF = [0, 512, 896, 1152]
SLEN = 1280


class Sched:

    def __init__(self, nc, sems):
        self.nc = nc
        self.sems = sems
        self.count = {k: 0 for k in sems}
        self.steps = []

    def val(self, sem):
        return self.count[sem]

    log = []

    def add(self, engine, fn, waits=(), inc=None, inc_amt=1):
        w = [(s, v) for (s, v) in waits if v is not None and v > 0]
        if inc is not None:
            self.count[inc] += inc_amt
        self.steps.append((engine, fn, w, inc, inc_amt))
        Sched.log.append((engine, tuple(w), inc, inc_amt))
        return self.count[inc] if inc is not None else None

    def flush(self):
        with self.nc.Block() as blk:
            for name in ("sync", "tensor", "vector", "scalar", "gpsimd"):
                if not any(s[0] == name for s in self.steps):
                    continue

                def make(nm):
                    def body(eng):
                        for (en, fn, waits, inc, amt) in self.steps:
                            if en != nm:
                                continue
                            for (s, v) in waits:
                                eng.wait_ge(self.sems[s], v)
                            ins = fn(eng)
                            if inc is not None:
                                ins.then_inc(self.sems[inc], amt)
                    return body
                getattr(blk, name)(make(name))
        self.steps = []


D16 = 16


def build_program(taps=(), stop_after=5, ngroups=NG, attn_sub='all'):
    nc = bass.Bass("TRN2", target_bir_lowering=False)

    w2tp = nc.dram_tensor("w2tp", [FF, D], F32, kind="ExternalInput").ap()
    wfq = nc.dram_tensor("wfq", [H, 64, FD], F32, kind="ExternalInput").ap()
    wfk = nc.dram_tensor("wfk", [H, 64, FD], F32, kind="ExternalInput").ap()
    x_in = nc.dram_tensor("x_in", [L, D], F32, kind="ExternalInput").ap()
    wkT = nc.dram_tensor("wkT", [D, D], F32, kind="ExternalInput").ap()
    wqT = nc.dram_tensor("wqT", [D, D], F32, kind="ExternalInput").ap()
    wvT = nc.dram_tensor("wvT", [D, D], F32, kind="ExternalInput").ap()
    woT = nc.dram_tensor("woT", [D, D], F32, kind="ExternalInput").ap()
    w1Tt = nc.dram_tensor("w1Tt", [NHC, 128, NDC, 128], F32, kind="ExternalInput").ap()
    w3Tt = nc.dram_tensor("w3Tt", [NHC, 128, NDC, 128], F32, kind="ExternalInput").ap()
    pad_in = nc.dram_tensor("pad_in", [4096, 1024], F32, kind="ExternalInput").ap()
    out = nc.dram_tensor("out", [L, D], F32, kind="ExternalOutput").ap()

    u_dram = nc.dram_tensor("u_scr", [128, 64, 2048], BF16, kind="Internal").ap()
    fl_dram = nc.dram_tensor("fl_scr", [128, 16], F32, kind="Internal").ap()
    w2p = nc.dram_tensor("w2p_scr", [NHC, 128, 2048], BF16, kind="Internal").ap()
    y_dram = nc.dram_tensor("y_scr", [NDC, NT, 128, 128], BF16, kind="Internal").ap()
    x1_dram = nc.dram_tensor("x1_scr", [L, D], F32, kind="Internal").ap()
    ln2_dram = nc.dram_tensor("ln2_scr", [NBLK, NDC, 128, 512], BF16, kind="Internal").ap()

    tap_h = {}

    def tap(name, shape, dtype=F32):
        if name in taps:
            tap_h[name] = nc.dram_tensor("tap_" + name, shape, dtype,
                                         kind="ExternalOutput").ap()
            return tap_h[name]
        return None

    stack = contextlib.ExitStack()
    sems = {n: stack.enter_context(nc.semaphore("sem_" + n))
            for n in ["gW", "sX", "sSt", "PE", "DV", "AC", "GP"]}
    S = Sched(nc, sems)
    _tapped = set()

    def emit_scratch_taps(avail):
        for name, src_ap, shape, dt in avail:
            if name in _tapped:
                continue
            t = tap(name, shape, dt)
            if t is not None:
                _tapped.add(name)
                S.add("sync",
                      (lambda t=t, src_ap=src_ap: lambda sy: sy.dma_start(
                          out=t, in_=src_ap))(),
                      waits=[("sSt", S.val("sSt"))], inc="sSt", inc_amt=D16)

    def sb(st, name, shape, dt=F32):
        return st.enter_context(nc.sbuf_tensor(name, shape, dt))

    def psum(st, name, dt=F32):
        return st.enter_context(nc.psum_tensor(name, [128, 512], dt))

    ident = sb(stack, "ident", [128, 128], BF16)
    mask_tri = sb(stack, "mask_tri", [128, 128], BF16)
    h128 = sb(stack, "h128", [128, 128], BF16)
    h64bd = sb(stack, "h64bd", [128, 128], BF16)
    wfq_bd = sb(stack, "wfq_bd", [128, 16 * 128], BF16)
    wfk_bd = sb(stack, "wfk_bd", [128, 16 * 128], BF16)
    x0_t = sb(stack, "x0_t", [128, D * 2], F32)
    lnst = sb(stack, "lnst", [128, 48], F32)
    lnmv = sb(stack, "lnmv", [128, 4], F32)
    ln_s1 = sb(stack, "ln_s1", [128, 2], F32)
    ln_s2 = sb(stack, "ln_s2", [128, 2], F32)
    eps_sb = sb(stack, "eps_sb", [128, 1], F32)
    flsb = sb(stack, "flsb", [128, 16], F32)

    def flush(queue, sem):
        S.add(queue, lambda e: e.dma_start(out=flsb[:], in_=fl_dram),
              inc=sem, inc_amt=D16)
        return S.val(sem)

    ph01 = contextlib.ExitStack()
    iop = sb(ph01, "iop", [128, 128], I32)
    iof = sb(ph01, "iof", [128, 128], I32)
    it1 = sb(ph01, "it1", [128, 128], I32)
    it2 = sb(ph01, "it2", [128, 128], I32)
    foldi = sb(ph01, "foldi", [128, 2048 * 4], BF16)
    foldo = sb(ph01, "foldo", [128, 2048 * 3], BF16)
    pbank = [psum(ph01, f"pf{i}") for i in range(4)]

    S.add("gpsimd", lambda g: g.iota(iop[:], pattern=[[0, 128]], base=0,
                                     channel_multiplier=1), inc="GP")
    S.add("gpsimd", lambda g: g.iota(iof[:], pattern=[[1, 128]], base=0,
                                     channel_multiplier=0), inc="GP")
    v_io = S.val("GP")
    S.add("vector", lambda v: v.tensor_tensor(ident[:], iop[:], iof[:], ALU.is_equal),
          waits=[("GP", v_io)])
    S.add("vector", lambda v: v.tensor_tensor(mask_tri[:], iof[:], iop[:], ALU.is_ge))
    S.add("vector", lambda v: v.tensor_tensor(it1[:], iop[:], iof[:], ALU.bitwise_and))
    for sh in (4, 2, 1):
        S.add("vector", (lambda sh=sh: lambda v: v.tensor_scalar(
            it2[:], it1[:], sh, None, ALU.arith_shift_right))())
        S.add("vector", lambda v: v.tensor_tensor(it1[:], it1[:], it2[:],
                                                  ALU.bitwise_xor))
    S.add("vector", lambda v: v.tensor_scalar(it2[:], it1[:], 1, None, ALU.bitwise_and))
    S.add("vector", lambda v: v.tensor_scalar(h128[:], it2[:], -2.0, 1.0,
                                              ALU.mult, ALU.add), inc="DV")
    v_h128 = S.val("DV")
    S.add("vector", lambda v: v.memset(h64bd[:], 0.0), inc="DV")
    v_memset = S.val("DV")
    S.add("vector", lambda v: v.tensor_copy(h64bd[0:64, 0:64], h128[0:64, 0:64]),
          waits=[("DV", v_h128), ("DV", v_memset)])
    S.add("vector", lambda v: v.tensor_copy(h64bd[64:128, 64:128], h128[0:64, 0:64]))
    S.add("vector", lambda v: v.memset(eps_sb[:], LN_EPS))
    S.add("vector", lambda v: v.memset(wfq_bd[:], 0.0))
    S.add("vector", lambda v: v.memset(wfk_bd[:], 0.0), inc="DV")
    v_const = S.val("DV")
    for pg in range(16):
        for (dst, srcw) in ((wfq_bd, wfq), (wfk_bd, wfk)):
            S.add("gpsimd",
                  (lambda dst=dst, srcw=srcw, pg=pg: lambda g: g.dma_start(
                      out=dst[0:64, pg * 128:pg * 128 + 64], in_=srcw[2 * pg]))(),
                  waits=[("DV", v_const)] if pg == 0 else [],
                  inc="gW", inc_amt=D16)
            S.add("gpsimd",
                  (lambda dst=dst, srcw=srcw, pg=pg: lambda g: g.dma_start(
                      out=dst[64:128, pg * 128 + 64:pg * 128 + 128],
                      in_=srcw[2 * pg + 1]))(),
                  inc="gW", inc_amt=D16)
    g_wf = flush("gpsimd", "gW")
    dv_wf = S.val("DV")

    fA_fl = {}
    fA_emitted = set()
    slot_mm = [0, 0, 0, 0]
    ev_done = [0, 0, 0, 0]
    st_done = [0, 0, 0]

    def emit_foldA_load(bp):
        if bp in fA_emitted or bp >= 64:
            return
        fA_emitted.add(bp)
        slot = bp % 4
        S.add("gpsimd",
              (lambda bp=bp, slot=slot: lambda g: g.dma_start(
                  out=foldi[:, slot * 2048:(slot + 1) * 2048],
                  in_=w2tp[2 * bp * 64:2 * bp * 64 + 128, :]))(),
              waits=[("PE", slot_mm[slot])], inc="gW", inc_amt=D16)
        fA_fl[bp] = flush("gpsimd", "gW")

    for bp in range(64):
        for _j in range(bp, bp + 3):
            emit_foldA_load(_j)
        slot = bp % 4
        sloto = bp % 3
        for dob in range(4):
            bank = pbank[dob]
            S.add("tensor",
                  (lambda bank=bank, slot=slot, dob=dob: lambda t: t.matmul(
                      bank[:, 0:512], h64bd[:],
                      foldi[:, slot * 2048 + dob * 512:slot * 2048 + (dob + 1) * 512],
                      start=True, stop=True))(),
                  waits=[("gW", fA_fl[min(bp + 2, 63)]), ("DV", v_const),
                         ("DV", ev_done[dob])],
                  inc="PE")
            mmc = S.val("PE")
            S.add("vector",
                  (lambda bank=bank, sloto=sloto, dob=dob: lambda v: v.tensor_copy(
                      foldo[:, sloto * 2048 + dob * 512:sloto * 2048 + (dob + 1) * 512],
                      bank[:, 0:512]))(),
                  waits=[("PE", mmc), ("sSt", st_done[sloto])], inc="DV")
            ev_done[dob] = S.val("DV")
        slot_mm[slot] = S.val("PE")
        ev4 = S.val("DV")
        S.add("sync",
              (lambda bp=bp, sloto=sloto: lambda sy: sy.dma_start(
                  out=u_dram[2 * bp],
                  in_=foldo[0:64, sloto * 2048:(sloto + 1) * 2048]))(),
              waits=[("DV", ev4)], inc="sSt", inc_amt=D16)
        S.add("sync",
              (lambda bp=bp, sloto=sloto: lambda sy: sy.dma_start(
                  out=u_dram[2 * bp + 1],
                  in_=foldo[64:128, sloto * 2048:(sloto + 1) * 2048]))(),
              inc="sSt", inc_amt=D16)
        S.add("sync", lambda sy: sy.dma_start(out=fl_dram, in_=flsb[:]),
              inc="sSt", inc_amt=D16)
        st_done[sloto] = S.val("sSt")
    stA = S.val("sSt")

    scaleW = 1.0 / math.sqrt(FF)
    fB_fl = {}
    fB_emitted = set()
    slot_mm = [0, 0, 0, 0]
    ev_done = [0, 0, 0, 0]
    st_done = [0, 0, 0]

    def emit_foldB_load(ai):
        if ai in fB_emitted or ai >= 64:
            return
        fB_emitted.add(ai)
        slot = ai % 4
        S.add("gpsimd",
              (lambda ai=ai, slot=slot: lambda g: g.dma_start(
                  out=foldi[:, slot * 2048:(slot + 1) * 2048],
                  in_=u_dram[:, ai, :]))(),
              waits=[("PE", slot_mm[slot])] +
                    ([("sSt", stA)] if ai == 0 else []),
              inc="gW", inc_amt=D16)
        fB_fl[ai] = flush("gpsimd", "gW")

    for ai in range(64):
        for _j in range(ai, ai + 3):
            emit_foldB_load(_j)
        slot = ai % 4
        sloto = ai % 3
        for dob in range(4):
            bank = pbank[dob]
            S.add("tensor",
                  (lambda bank=bank, slot=slot, dob=dob: lambda t: t.matmul(
                      bank[:, 0:512], h128[:],
                      foldi[:, slot * 2048 + dob * 512:slot * 2048 + (dob + 1) * 512],
                      start=True, stop=True))(),
                  waits=[("gW", fB_fl[min(ai + 2, 63)]), ("DV", ev_done[dob])],
                  inc="PE")
            mmc = S.val("PE")
            S.add("vector",
                  (lambda bank=bank, sloto=sloto, dob=dob: lambda v: v.tensor_scalar(
                      foldo[:, sloto * 2048 + dob * 512:sloto * 2048 + (dob + 1) * 512],
                      bank[:, 0:512], scaleW, None, ALU.mult))(),
                  waits=[("PE", mmc), ("sSt", st_done[sloto])], inc="DV")
            ev_done[dob] = S.val("DV")
        slot_mm[slot] = S.val("PE")
        ev4 = S.val("DV")
        S.add("sync",
              (lambda ai=ai, sloto=sloto: lambda sy: sy.dma_start(
                  out=w2p[ai],
                  in_=foldo[:, sloto * 2048:(sloto + 1) * 2048]))(),
              waits=[("DV", ev4)], inc="sSt", inc_amt=D16)
        S.add("sync", lambda sy: sy.dma_start(out=fl_dram, in_=flsb[:]),
              inc="sSt", inc_amt=D16)
        st_done[sloto] = S.val("sSt")
    w2p_all_st = S.val("sSt")
    S.flush()
    ph01.close()

    if stop_after <= 1:
        emit_scratch_taps([("w2p", w2p, [NHC, 128, 2048], BF16),
                           ("u", u_dram, [128, 64, 2048], BF16)])
        S.add("sync", lambda sy: sy.nop(),
              waits=[("sSt", S.val("sSt")), ("sX", S.val("sX")),
                     ("gW", S.val("gW"))])
        S.flush()
        stack.close()
        return nc, tap_h

    ph23 = contextlib.ExitStack()
    ln1_fm = sb(ph23, "ln1_fm", [128, NDC * L], BF16)
    ln1_tm = sb(ph23, "ln1_tm", [128, D * 2], BF16)
    wq_g = sb(ph23, "wq_g", [128, NDC * 256], BF16)
    wk_g = sb(ph23, "wk_g", [128, NDC * 256], BF16)
    wv_g = sb(ph23, "wv_g", [128, NDC * 256], BF16)
    qk_sb = sb(ph23, "qk_sb", [128, 512 * 2], BF16)
    eq_t = sb(ph23, "eq_t", [128, 128 * 4], BF16)
    ek_strip = sb(ph23, "ek_strip", [128, 2 * L], BF16)
    fq_fm = sb(ph23, "fq_fm", [128, 4 * L], BF16)
    fk_fm = sb(ph23, "fk_fm", [128, 4 * L], BF16)
    v65 = sb(ph23, "v65", [128, NT * 260], BF16)
    rk = sb(ph23, "rk", [128, NT * 4], F32)
    ksum = sb(ph23, "ksum", [128, 4], F32)
    s_sb = sb(ph23, "s_sb", [128, 4 * SLEN], BF16)
    st_f32 = sb(ph23, "st_f32", [128, 130], F32)
    st_pad = sb(ph23, "st_pad", [128, 260], BF16)
    y_tm = sb(ph23, "y_tm", [128, 128 * 2], BF16)
    y_strip = sb(ph23, "y_strip", [128, 2 * 512], BF16)
    den_t = sb(ph23, "den_t", [128, 4], F32)
    rec_t = sb(ph23, "rec_t", [128, 4], F32)
    pA = psum(ph23, "pA")
    pB = psum(ph23, "pB")
    pC = psum(ph23, "pC")
    pD = psum(ph23, "pD")
    pE = psum(ph23, "pE")
    pF = psum(ph23, "pF")
    tpP = [psum(ph23, "tp0", BF16), psum(ph23, "tp1", BF16)]

    ln_sqrt_ac = [0, 0]

    def ln_stats(src_fn, first_waits, par=0):
        for j in range(4):
            S.add("vector",
                  (lambda j=j, par=par: lambda v: v.bn_stats(
                      lnst[:, 24 * par + 6 * j:24 * par + 6 * j + 6],
                      src_fn(j)))(),
                  waits=(list(first_waits) + [("AC", ln_sqrt_ac[par])])
                  if j == 0 else [], inc="DV")
        v4 = S.val("DV")
        S.add("vector",
              (lambda par=par: lambda v: v.bn_aggr(
                  lnmv[:, 2 * par:2 * par + 2],
                  lnst[:, 24 * par:24 * par + 24]))(),
              waits=[("DV", v4)], inc="DV")
        return S.val("DV")

    def ln_finish(par, va):
        S.add("scalar",
              (lambda par=par: lambda s: s.activation(
                  ln_s2[:, par:par + 1], lnmv[:, 2 * par + 1:2 * par + 2],
                  AF.Sqrt, bias=eps_sb[:], scale=1.0))(),
              waits=[("DV", va)], inc="AC")
        ln_sqrt_ac[par] = S.val("AC")
        vs = S.val("AC")
        S.add("vector",
              (lambda par=par: lambda v: v.reciprocal(
                  ln_s2[:, par:par + 1], ln_s2[:, par:par + 1]))(),
              waits=[("AC", vs)], inc="DV")
        vr = S.val("DV")
        S.add("vector",
              (lambda par=par: lambda v: v.tensor_scalar(
                  ln_s1[:, par:par + 1], lnmv[:, 2 * par:2 * par + 1], -1.0,
                  None, ALU.mult))(),
              waits=[("DV", vr)], inc="DV")
        return S.val("DV")

    def emit_ln_stats(src_fn, first_waits, par=0):
        va = ln_stats(src_fn, first_waits, par)
        return ln_finish(par, va)

    tp_free = [0, 0]
    apply_done = [0, 0]
    ln1tm_free = [0, 0]
    stage2 = {}
    for tc in range(NT + 1):
        if tc < NT:
            slot = tc % 2
            S.add("sync",
                  (lambda tc=tc, slot=slot: lambda sy: sy.dma_start(
                      out=x0_t[:, slot * D:(slot + 1) * D],
                      in_=x_in[tc * 128:(tc + 1) * 128, :]))(),
                  waits=[("DV", apply_done[slot])], inc="sX", inc_amt=D16)
            x_ld = flush("sync", "sX")
            va = ln_stats(
                (lambda slot=slot: lambda j:
                 x0_t[:, slot * D + 512 * j:slot * D + 512 * (j + 1)])(),
                [("sX", x_ld)], par=slot)
            stage2[tc] = va
        if tc >= 1:
            tp_ = tc - 1
            slot = tp_ % 2
            vstats = ln_finish(slot, stage2[tp_])
            S.add("vector",
                  (lambda slot=slot: lambda v: v.tensor_scalar(
                      ln1_tm[:, slot * D:(slot + 1) * D],
                      x0_t[:, slot * D:(slot + 1) * D],
                      ln_s1[:, slot:slot + 1], ln_s2[:, slot:slot + 1],
                      ALU.add, ALU.mult))(),
                  waits=[("DV", vstats), ("PE", ln1tm_free[slot])], inc="DV")
            apply_done[slot] = S.val("DV")
            for q4 in range(4):
                hp = q4 % 2
                for j in range(4):
                    dc = q4 * 4 + j
                    S.add("tensor",
                          (lambda slot=slot, dc=dc, hp=hp, j=j: lambda t: t.transpose(
                              tpP[hp][:, j * 128:(j + 1) * 128],
                              ln1_tm[:, slot * D + dc * 128:slot * D + (dc + 1) * 128],
                              ident[:]))(),
                          waits=([("DV", apply_done[slot]), ("DV", tp_free[hp])]
                                 if j == 0 else []),
                          inc="PE" if j == 3 else None)
                pe_tp = S.val("PE")
                S.add("vector",
                      (lambda tc2=tp_, q4=q4, hp=hp: lambda v: v.tensor_copy(
                          ln1_fm.ap().rearrange("p (dc t) -> p dc t", t=L)[
                              :, q4 * 4:(q4 + 1) * 4, tc2 * 128:(tc2 + 1) * 128],
                          tpP[hp][:, 0:512].rearrange(
                              "p (dc t) -> p dc t", t=128)))(),
                      waits=[("PE", pe_tp)], inc="DV")
                tp_free[hp] = S.val("DV")
            ln1tm_free[slot] = S.val("PE")
        ln1_done = S.val("DV")

    t_ln1 = tap("ln1_fm", [128, NDC * L], BF16)
    if t_ln1 is not None:
        S.add("sync", lambda sy: sy.dma_start(out=t_ln1, in_=ln1_fm[:]),
              waits=[("DV", ln1_done)], inc="sSt", inc_amt=D16)

    if stop_after <= 2:
        S.add("sync", lambda sy: sy.nop(),
              waits=[("sSt", S.val("sSt")), ("sX", S.val("sX")),
                     ("gW", S.val("gW")), ("DV", ln1_done)])
        S.flush()
        ph23.close()
        stack.close()
        return nc, tap_h

    S.add("vector", lambda v: v.memset(fq_fm[:], 0.0))
    S.add("vector", lambda v: v.memset(fk_fm[:], 0.0), inc="DV")
    strips_z = S.val("DV")
    prev_group_pe = 0
    prev_proj_pe = 0
    ystrip_st = [0, 0]
    for g in range(ngroups):
        wld = {}
        for (wbuf, wsrc) in ((wk_g, wkT), (wq_g, wqT), (wv_g, wvT)):
            for dc in range(NDC):
                S.add("gpsimd",
                      (lambda wbuf=wbuf, wsrc=wsrc, dc=dc, g=g: lambda gp:
                       gp.dma_start(
                           out=wbuf[:, dc * 256:(dc + 1) * 256],
                           in_=wsrc[dc * 128:(dc + 1) * 128,
                                    g * 256:(g + 1) * 256]))(),
                      waits=[("PE", prev_proj_pe)] if dc == 0 else [],
                      inc="gW", inc_amt=D16)
            wld[id(wbuf)] = flush("gpsimd", "gW")

        qkbank = [pA, pB]
        zbank = [pC, pD]
        qk_free = [0, 0]
        z_free = [0, 0]
        tsc_tp_pe = {}

        def proj_feat(which, wbuf, wfbd, dst_fm, g=g, wld=wld,
                      tsc_tp_pe=tsc_tp_pe, qk_free=qk_free, z_free=z_free):
            for p in range(2):
                for tsc in range(4):
                    u = p * 4 + tsc
                    slot = u % 2
                    bank = qkbank[slot]
                    for dc in range(NDC):
                        S.add("tensor",
                              (lambda bank=bank, wbuf=wbuf, dc=dc, p=p, tsc=tsc:
                               lambda t: t.matmul(
                                   bank[:, 0:512],
                                   wbuf[:, dc * 256 + p * 128:dc * 256 + (p + 1) * 128],
                                   ln1_fm[:, dc * L + tsc * 512:dc * L + (tsc + 1) * 512],
                                   start=(dc == 0), stop=(dc == NDC - 1)))(),
                              waits=([("gW", wld[id(wbuf)]), ("DV", ln1_done),
                                      ("AC", qk_free[slot])] if dc == 0 else []),
                              inc="PE" if dc == NDC - 1 else None)
                    pe_p = S.val("PE")
                    S.add("scalar",
                          (lambda bank=bank, slot=slot: lambda s: s.copy(
                              qk_sb[:, slot * 512:(slot + 1) * 512],
                              bank[:, 0:512]))(),
                          waits=[("PE", pe_p)], inc="AC")
                    ac_ev = S.val("AC")
                    qk_free[slot] = ac_ev
                    exp_acs = []
                    prev_tp = tsc_tp_pe.get((which, p, tsc - 1), 0)
                    if which == "q":
                        prev_tp = max(prev_tp, tsc_tp_pe.get(("q", p - 1, 3), 0))
                    else:
                        prev_tp = max(prev_tp, tsc_tp_pe.get(("k", p - 1, 3), 0))
                    for tci in range(4):
                        tc = tsc * 4 + tci
                        zb = zbank[tci % 2]
                        S.add("tensor",
                              (lambda zb=zb, slot=slot, tci=tci, wfbd=wfbd,
                                      g=g, p=p: lambda t: t.matmul(
                                  zb[:, 0:128],
                                  qk_sb[:, slot * 512 + tci * 128:
                                        slot * 512 + (tci + 1) * 128],
                                  wfbd[:, (2 * g + p) * 128:(2 * g + p + 1) * 128],
                                  start=True, stop=True))(),
                              waits=[("AC", ac_ev), ("gW", g_wf),
                                     ("AC", z_free[tci % 2])],
                              inc="PE")
                        pe_z = S.val("PE")
                        if which == "k":
                            dst = (lambda p=p, tc=tc:
                                   ek_strip[:, p * L + tc * 128:p * L + (tc + 1) * 128])
                        else:
                            dst = (lambda tci=tci:
                                   eq_t[:, tci * 128:(tci + 1) * 128])
                        S.add("scalar",
                              (lambda dst=dst, zb=zb: lambda s: s.activation(
                                  dst(), zb[:, 0:128], AF.Exp, bias=0.0,
                                  scale=1.0))(),
                              waits=[("PE", pe_z)] +
                                    ([("PE", prev_tp)] if (which == "q" and tci == 0)
                                     else []),
                              inc="AC")
                        z_free[tci % 2] = S.val("AC")
                        exp_acs.append(S.val("AC"))
                        if which == "k":
                            S.add("vector",
                                  (lambda p=p, tc=tc: lambda v: v.reduce_sum(
                                      ksum[:, 0:2],
                                      ek_strip[:, p * L + tc * 128:
                                               p * L + (tc + 1) * 128]
                                      .rearrange("p (hh f) -> p hh f", hh=2),
                                      axis=mybir.AxisListType.X))(),
                                  waits=[("AC", S.val("AC"))], inc="DV")
                            vks = S.val("DV")
                            S.add("vector",
                                  (lambda p=p, tc=tc: lambda v: v.reciprocal(
                                      rk[:, tc * 4 + 2 * p:tc * 4 + 2 * p + 2],
                                      ksum[:, 0:2]))(),
                                  waits=[("DV", vks)], inc="DV")
                    tph = u % 2
                    for tci in range(4):
                        tc = tsc * 4 + tci
                        if which == "k":
                            src = (lambda p=p, tc=tc:
                                   ek_strip[:, p * L + tc * 128:p * L + (tc + 1) * 128])
                        else:
                            src = (lambda tci=tci:
                                   eq_t[:, tci * 128:(tci + 1) * 128])
                        S.add("tensor",
                              (lambda src=src, tph=tph, tci=tci: lambda t:
                               t.transpose(
                                   tpP[tph][:, tci * 128:(tci + 1) * 128],
                                   src(), ident[:]))(),
                              waits=([("AC", exp_acs[3]), ("DV", tp_free[tph])]
                                     if tci == 0 else []),
                              inc="PE" if tci == 3 else None)
                    pe_tp4 = S.val("PE")
                    tsc_tp_pe[(which, p, tsc)] = pe_tp4
                    for par in range(2):
                        hstrip = 2 * p + par
                        S.add("vector",
                              (lambda dst_fm=dst_fm, hstrip=hstrip, par=par,
                                      tsc=tsc, tph=tph: lambda v: v.tensor_copy(
                                  dst_fm[64 * par:64 * par + 64,
                                         hstrip * L + tsc * 512:
                                         hstrip * L + (tsc + 1) * 512],
                                  tpP[tph][64 * par:64 * par + 64, 0:512]))(),
                              waits=[("PE", pe_tp4), ("DV", strips_z)]
                              if par == 0 else [],
                              inc="DV" if par == 1 else None)
                    tp_free[tph] = S.val("DV")
            return

        proj_feat("k", wk_g, wfk_bd, fk_fm)
        k_done = S.val("DV")
        if attn_sub == "k":
            break
        proj_feat("q", wq_g, wfq_bd, fq_fm)
        q_done = S.val("DV")
        q_ac = S.val("AC")
        if attn_sub == "kq":
            break

        v_free = [0, 0]
        for tc in range(NT):
            slot = tc % 2
            bank = qkbank[slot]
            for dc in range(NDC):
                S.add("tensor",
                      (lambda bank=bank, dc=dc, tc=tc, g=g: lambda t: t.matmul(
                          bank[:, 0:256],
                          ln1_fm[:, dc * L + tc * 128:dc * L + (tc + 1) * 128],
                          wv_g[:, dc * 256:(dc + 1) * 256],
                          start=(dc == 0), stop=(dc == NDC - 1)))(),
                      waits=([("gW", wld[id(wv_g)]), ("DV", k_done),
                              ("AC", q_ac),
                              ("DV", v_free[slot])] if dc == 0 else []),
                      inc="PE" if dc == NDC - 1 else None)
            pe_v = S.val("PE")
            for hh in range(4):
                S.add("vector",
                      (lambda bank=bank, tc=tc, hh=hh: lambda v: v.tensor_scalar(
                          v65[:, tc * 260 + hh * 65:tc * 260 + hh * 65 + 64],
                          bank[:, hh * 64:(hh + 1) * 64],
                          rk[:, tc * 4 + hh:tc * 4 + hh + 1], None, ALU.mult))(),
                      waits=[("PE", pe_v)] if hh == 0 else [])
            S.add("vector",
                  (lambda tc=tc: lambda v: v.tensor_copy(
                      v65.ap().rearrange("p (t hh c) -> p t hh c", t=NT, hh=4)[
                          :, tc, :, 64:65].rearrange("p hh c -> p (hh c)"),
                      rk[:, tc * 4:(tc + 1) * 4]))(),
                  inc="DV")
            v_free[slot] = S.val("DV")
        v_done = S.val("DV")
        prev_proj_pe = S.val("PE")
        if attn_sub == "kqv":
            break

        S.add("vector", lambda v: v.memset(st_f32[:], 0.0),
              waits=[("PE", prev_group_pe)])
        S.add("vector", lambda v: v.memset(st_pad[:], 0.0), inc="DV")
        st_ready = S.val("DV")

        sbank = [pA, pB]
        ybank = [pC, pD]
        stbank = pE
        s_ev = {}
        prev_y_pe = 0
        ytm_free_pe = 0
        ybank_free_dv = 0
        for sc in range(NSC):
            sb_free = s_ev.get("free", [(0, 0), (0, 0)])
            it_s = 0
            for hh in range(4):
                p, par = hh // 2, hh % 2
                for j in range(4):
                    bslot_s = it_s % 2
                    bank = sbank[bslot_s]
                    tk = sc * 4 + j
                    nj = 512 - j * 128
                    w0 = [("AC", sb_free[bslot_s][0]), ("DV", sb_free[bslot_s][1])]
                    if it_s == 0:
                        w0 += [("DV", v_done), ("PE", prev_y_pe)]
                    S.add("tensor",
                          (lambda bank=bank, hh=hh, tk=tk, j=j, sc=sc,
                                  nj=nj: lambda t: t.matmul(
                              bank[:, 0:nj],
                              fk_fm[:, hh * L + tk * 128:hh * L + (tk + 1) * 128],
                              fq_fm[:, hh * L + sc * 512 + j * 128:
                                    hh * L + (sc + 1) * 512],
                              start=True, stop=True))(),
                          waits=w0, inc="PE")
                    pe_s = S.val("PE")
                    base = hh * SLEN + SOFF[j]
                    S.add("vector",
                          (lambda bank=bank, base=base: lambda v: v.tensor_tensor(
                              s_sb[:, base:base + 128], bank[:, 0:128],
                              mask_tri[:], ALU.mult))(),
                          waits=[("PE", pe_s)],
                          inc=None if nj > 128 else "DV")
                    if nj > 128:
                        S.add("vector",
                              (lambda bank=bank, base=base, nj=nj: lambda v:
                               v.tensor_copy(s_sb[:, base + 128:base + nj],
                                             bank[:, 128:nj]))(),
                              inc="DV")
                    sb_free[bslot_s] = (S.val("AC"), S.val("DV"))
                    it_s += 1
            s_ev["free"] = sb_free
            s_ac, s_dv = S.val("AC"), S.val("DV")
            if attn_sub == "s":
                prev_y_pe = S.val("PE")
                continue

            for i in range(4):
                tq = sc * 4 + i
                for hh in range(4):
                    p, par = hh // 2, hh % 2
                    bank = ybank[par]
                    S.add("tensor",
                          (lambda bank=bank, p=p, hh=hh, tq=tq, st=(hh < 2):
                           lambda t: t.matmul(
                               bank[:, 65 * p:65 * p + 65],
                               fq_fm[:, hh * L + tq * 128:hh * L + (tq + 1) * 128],
                               st_pad[:, hh * 65:hh * 65 + 65],
                               start=st, stop=False))(),
                          waits=([("DV", st_ready), ("AC", s_ac), ("DV", s_dv),
                                  ("DV", ybank_free_dv)] if hh == 0 else []))
                for hh in range(4):
                    p, par = hh // 2, hh % 2
                    bank = ybank[par]
                    for j in range(i + 1):
                        off = hh * SLEN + SOFF[j] + (i - j) * 128
                        last = (j == i) and (hh >= 2)
                        S.add("tensor",
                              (lambda bank=bank, p=p, off=off, hh=hh, sc=sc,
                                      j=j, last=last: lambda t: t.matmul(
                                  bank[:, 65 * p:65 * p + 65],
                                  s_sb[:, off:off + 128],
                                  v65[:, (sc * 4 + j) * 260 + hh * 65:
                                      (sc * 4 + j) * 260 + hh * 65 + 65],
                                  start=False, stop=last))(),
                              inc="PE" if (j == i and hh == 3) else None)
                pe_y = S.val("PE")
                prev_y_pe = pe_y
                S.add("vector",
                      lambda v: v.tensor_scalar(den_t[:, 0:2], pC[:, 64:130:65],
                                                DEN_EPS, None, ALU.add),
                      waits=[("PE", pe_y)])
                S.add("vector",
                      lambda v: v.tensor_scalar(den_t[:, 2:4], pD[:, 64:130:65],
                                                DEN_EPS, None, ALU.add),
                      inc="DV")
                v_den = S.val("DV")
                S.add("vector", lambda v: v.reciprocal(rec_t[:], den_t[:]),
                      waits=[("DV", v_den)], inc="DV")
                v_rec = S.val("DV")
                for hh in range(4):
                    p, par = hh // 2, hh % 2
                    bank = ybank[par]
                    S.add("vector",
                          (lambda bank=bank, p=p, par=par, ridx=2 * par + p:
                           lambda v: v.tensor_scalar(
                               y_tm[:, p * 128 + par * 64:p * 128 + par * 64 + 64],
                               bank[:, 65 * p:65 * p + 64],
                               rec_t[:, ridx:ridx + 1], None, ALU.mult))(),
                          waits=([("DV", v_rec), ("PE", ytm_free_pe)]
                                 if hh == 0 else []),
                          inc="DV" if hh == 3 else None)
                v_ytm = S.val("DV")
                ybank_free_dv = v_ytm
                tph = i % 2
                for p in range(2):
                    S.add("tensor",
                          (lambda p=p, tph=tph, i=i: lambda t: t.transpose(
                              tpP[tph][:, ((2 * i) % 4 + p) * 128:
                                       ((2 * i) % 4 + p + 1) * 128],
                              y_tm[:, p * 128:(p + 1) * 128], ident[:]))(),
                          waits=([("DV", v_ytm), ("DV", tp_free[tph])]
                                 if p == 0 else []),
                          inc="PE" if p == 1 else None)
                pe_ytp = S.val("PE")
                ytm_free_pe = pe_ytp
                for p in range(2):
                    wv = [("PE", pe_ytp)]
                    if i == 0:
                        wv.append(("sSt", ystrip_st[p]))
                    S.add("vector",
                          (lambda p=p, tph=tph, i=i: lambda v: v.tensor_copy(
                              y_strip[:, p * 512 + i * 128:p * 512 + (i + 1) * 128],
                              tpP[tph][:, ((2 * i) % 4 + p) * 128:
                                       ((2 * i) % 4 + p + 1) * 128]))(),
                          waits=wv, inc="DV" if p == 1 else None)
                tp_free[tph] = S.val("DV")
            y_ev = S.val("DV")
            if attn_sub == "sy":
                for p in range(2):
                    for tci4 in range(4):
                        S.add("sync",
                              (lambda g=g, p=p, sc=sc, tci4=tci4: lambda sy:
                               sy.dma_start(
                                   out=y_dram[2 * g + p, sc * 4 + tci4],
                                   in_=y_strip[:, p * 512 + tci4 * 128:
                                               p * 512 + (tci4 + 1) * 128]))(),
                              waits=[("DV", y_ev)] if tci4 == 0 else [],
                              inc="sSt", inc_amt=D16)
                    ystrip_st[p] = S.val("sSt")
                continue

            first = True
            for hh in range(4):
                p, par = hh // 2, hh % 2
                for j in range(4):
                    tk = sc * 4 + j
                    last = (hh == 3 and j == 3)
                    S.add("tensor",
                          (lambda p=p, par=par, tk=tk, hh=hh, st0=(j == 0),
                                  last=last: lambda t: t.matmul(
                              stbank[64 * par:64 * par + 64, 65 * p:65 * p + 65],
                              ek_strip[:, p * L + tk * 128 + 64 * par:
                                       p * L + tk * 128 + 64 * par + 64],
                              v65[:, tk * 260 + hh * 65:tk * 260 + (hh + 1) * 65],
                              start=st0, stop=last, skip_group_check=True))(),
                          waits=[("DV", st_ready)] if first else [],
                          inc="PE" if last else None)
                    first = False
            pe_st = S.val("PE")
            S.add("vector",
                  lambda v: v.tensor_add(st_f32[:, 0:130], st_f32[:, 0:130],
                                         stbank[:, 0:130]),
                  waits=[("PE", pe_st)], inc="DV")
            v_sta = S.val("DV")
            for hh in range(4):
                p, par = hh // 2, hh % 2
                S.add("vector",
                      (lambda p=p, par=par, hh=hh: lambda v: v.tensor_copy(
                          st_pad[64 * par:64 * par + 64, hh * 65:hh * 65 + 65],
                          st_f32[64 * par:64 * par + 64, 65 * p:65 * p + 65]))(),
                      waits=[("DV", v_sta)] if hh == 0 else [],
                      inc="DV" if hh == 3 else None)
            st_ready = S.val("DV")

            for p in range(2):
                for tci4 in range(4):
                    S.add("sync",
                          (lambda g=g, p=p, sc=sc, tci4=tci4: lambda sy:
                           sy.dma_start(
                               out=y_dram[2 * g + p, sc * 4 + tci4],
                               in_=y_strip[:, p * 512 + tci4 * 128:
                                           p * 512 + (tci4 + 1) * 128]))(),
                          waits=[("DV", y_ev)] if tci4 == 0 else [],
                          inc="sSt", inc_amt=D16)
                ystrip_st[p] = S.val("sSt")
        prev_group_pe = S.val("PE")
    y_all_st = S.val("sSt")
    for nm, buf in (("fkfm", fk_fm), ("fqfm", fq_fm), ("ek", ek_strip),
                    ("v65t", v65), ("rkt", rk)):
        t = tap(nm, [128, buf.shape[1]], BF16 if nm != "rkt" else F32)
        if t is not None:
            S.add("sync",
                  (lambda t=t, buf=buf: lambda sy: sy.dma_start(out=t, in_=buf[:]))(),
                  waits=[("DV", S.val("DV")), ("AC", S.val("AC")),
                         ("PE", S.val("PE"))],
                  inc="sSt", inc_amt=D16)
    S.flush()
    ph23.close()

    if stop_after <= 3:
        emit_scratch_taps([("w2p", w2p, [NHC, 128, 2048], BF16), ("y", y_dram, [NDC, NT, 128, 128], BF16)])
        S.add("sync", lambda sy: sy.nop(),
              waits=[("sSt", S.val("sSt")), ("sX", S.val("sX")),
                     ("gW", S.val("gW"))])
        S.flush()
        stack.close()
        return nc, tap_h

    ph4 = contextlib.ExitStack()
    woT_sb = sb(ph4, "woT_sb", [128, NDC * 2048], BF16)
    y_t = sb(ph4, "y_t", [128, 2048 * 2], BF16)
    ao_sb = sb(ph4, "ao_sb", [128, D], F32)
    x1_sb = sb(ph4, "x1_sb", [128, D], F32)
    ln2_tm = sb(ph4, "ln2_tm", [128, D], BF16)
    stg = sb(ph4, "stg", [128, 512 * 2], BF16)
    pW = [psum(ph4, f"pw{i}") for i in range(4)]
    tpP = [psum(ph4, "tp0b", BF16), psum(ph4, "tp1b", BF16)]
    tp_free = [0, 0]

    for dc in range(NDC):
        for dob in range(4):
            S.add("gpsimd",
                  (lambda dc=dc, dob=dob: lambda gp: gp.dma_start(
                      out=woT_sb[:, dc * 2048 + dob * 512:dc * 2048 + (dob + 1) * 512],
                      in_=woT[dc * 128:(dc + 1) * 128, dob * 512:(dob + 1) * 512]))(),
                  inc="gW", inc_amt=D16)
    g_wo = flush("gpsimd", "gW")

    y_used = [0, 0]
    x0_used = [0, 0]
    bank_free = [0] * 4
    stg_free = [0, 0]
    ao_free = 0
    x1st_prev = 0
    ln2tm_free = 0
    for tc in range(NT):
        slot = tc % 2
        for k in range(NDC):
            S.add("sync",
                  (lambda tc=tc, slot=slot, k=k: lambda sy: sy.dma_start(
                      out=y_t[:, slot * 2048 + k * 128:slot * 2048 + (k + 1) * 128],
                      in_=y_dram[k, tc]))(),
                  waits=([("sSt", y_all_st), ("PE", y_used[slot])]
                         if k == 0 else []),
                  inc="sX", inc_amt=D16)
        y_ld = flush("sync", "sX")
        S.add("sync",
              (lambda tc=tc, slot=slot: lambda sy: sy.dma_start(
                  out=x0_t[:, slot * D:(slot + 1) * D],
                  in_=x_in[tc * 128:(tc + 1) * 128, :]))(),
              waits=[("DV", x0_used[slot])], inc="sX", inc_amt=D16)
        x0_ld = flush("sync", "sX")
        for dob in range(4):
            bank = pW[dob]
            for k in range(NDC):
                S.add("tensor",
                      (lambda bank=bank, k=k, dob=dob, slot=slot: lambda t:
                       t.matmul(
                           bank[:, 0:512],
                           y_t[:, slot * 2048 + k * 128:slot * 2048 + (k + 1) * 128],
                           woT_sb[:, k * 2048 + dob * 512:k * 2048 + (dob + 1) * 512],
                           start=(k == 0), stop=(k == NDC - 1)))(),
                      waits=([("sX", y_ld), ("gW", g_wo), ("AC", bank_free[dob])]
                             if k == 0 else []),
                      inc="PE" if k == NDC - 1 else None)
            pe_wo = S.val("PE")
            S.add("scalar",
                  (lambda bank=bank, dob=dob: lambda s: s.copy(
                      ao_sb[:, dob * 512:(dob + 1) * 512], bank[:, 0:512]))(),
                  waits=[("PE", pe_wo)] + ([("DV", ao_free)] if dob == 0 else []),
                  inc="AC")
            bank_free[dob] = S.val("AC")
        y_used[slot] = S.val("PE")
        ac_ao = S.val("AC")
        vst = emit_ln_stats(lambda j: ao_sb[:, 512 * j:512 * (j + 1)],
                            [("AC", ac_ao)])
        S.add("vector",
              lambda v: v.tensor_scalar(ao_sb[:], ao_sb[:], ln_s1[:, 0:1], ln_s2[:, 0:1],
                                        ALU.add, ALU.mult),
              waits=[("DV", vst)])
        S.add("vector",
              (lambda slot=slot: lambda v: v.tensor_add(
                  x1_sb[:], ao_sb[:], x0_t[:, slot * D:(slot + 1) * D]))(),
              waits=[("sX", x0_ld), ("sSt", x1st_prev)], inc="DV")
        x0_used[slot] = S.val("DV")
        v_x1 = S.val("DV")
        ao_free = v_x1
        S.add("sync",
              (lambda tc=tc: lambda sy: sy.dma_start(
                  out=x1_dram[tc * 128:(tc + 1) * 128, :], in_=x1_sb[:]))(),
              waits=[("DV", v_x1)], inc="sSt", inc_amt=D16)
        x1st_prev = S.val("sSt")
        vst2 = emit_ln_stats(lambda j: x1_sb[:, 512 * j:512 * (j + 1)], [])
        S.add("vector",
              lambda v: v.tensor_scalar(ln2_tm[:], x1_sb[:], ln_s1[:, 0:1], ln_s2[:, 0:1],
                                        ALU.add, ALU.mult),
              waits=[("DV", vst2), ("PE", ln2tm_free)], inc="DV")
        v_ln2 = S.val("DV")
        for q4 in range(4):
            hp = q4 % 2
            for j in range(4):
                dc = q4 * 4 + j
                S.add("tensor",
                      (lambda dc=dc, hp=hp, j=j: lambda t: t.transpose(
                          tpP[hp][:, j * 128:(j + 1) * 128],
                          ln2_tm[:, dc * 128:(dc + 1) * 128], ident[:]))(),
                      waits=([("DV", v_ln2), ("DV", tp_free[hp])]
                             if j == 0 else []),
                      inc="PE" if j == 3 else None)
            pe_tp = S.val("PE")
            S.add("vector",
                  (lambda hp=hp: lambda v: v.tensor_copy(
                      stg[:, hp * 512:(hp + 1) * 512], tpP[hp][:, 0:512]))(),
                  waits=[("PE", pe_tp), ("sSt", stg_free[hp])], inc="DV")
            tp_free[hp] = S.val("DV")
            v_stg = S.val("DV")
            blk_i, ci = tc // 4, tc % 4
            for j4 in range(4):
                S.add("sync",
                      (lambda blk_i=blk_i, ci=ci, q4=q4, hp=hp, j4=j4:
                       lambda sy: sy.dma_start(
                           out=ln2_dram[blk_i, q4 * 4 + j4, :,
                                        ci * 128:(ci + 1) * 128],
                           in_=stg[:, hp * 512 + j4 * 128:
                                   hp * 512 + (j4 + 1) * 128]))(),
                      waits=[("DV", v_stg)] if j4 == 0 else [],
                      inc="sSt", inc_amt=D16)
            stg_free[hp] = S.val("sSt")
        ln2tm_free = S.val("PE")
    ln2_all_st = S.val("sSt")
    x1_all_st = ln2_all_st
    S.flush()
    ph4.close()

    if stop_after <= 4:
        emit_scratch_taps([("w2p", w2p, [NHC, 128, 2048], BF16), ("y", y_dram, [NDC, NT, 128, 128], BF16), ("x1", x1_dram, [L, D], F32), ("ln2", ln2_dram, [NBLK, NDC, 128, 512], BF16)])
        S.add("sync", lambda sy: sy.nop(),
              waits=[("sSt", S.val("sSt")), ("sX", S.val("sX")),
                     ("gW", S.val("gW"))])
        S.flush()
        stack.close()
        return nc, tap_h

    ph5 = contextlib.ExitStack()
    ln2_blk = sb(ph5, "ln2_blk", [128, NDC * 512], BF16)
    h_fm = sb(ph5, "h_fm", [128, NHC * 512], BF16)
    w13_sb = sb(ph5, "w13_sb", [128, 4096 * 3], BF16)
    w2p_sb = sb(ph5, "w2p_sb", [128, 1024 * 12], BF16)
    silu_sb = sb(ph5, "silu_sb", [128, 512 * 2], F32)
    fo_sb = sb(ph5, "fo_sb", [128, 4 * D], F32)
    x1_t = sb(ph5, "x1_t", [128, D * 2], F32)
    wecho = None
    pU = [psum(ph5, f"pu{i}") for i in range(8)]

    ln2_ld = 0
    ln2_used = 0
    w13_ld = [0, 0, 0]
    w13_used = [0, 0, 0]
    w2p_fl = {}
    w2p_used6 = [0] * 12
    w2p_emitted = set()
    silu_free = [0, 0]
    h_used = 0
    ac_fo_prev = 0
    out_st = 0
    x1_used = [0, 0]
    fo_free = [0] * 8
    last_out = 0
    def emit_w2p_load(j, blk_i):
        if j in w2p_emitted or j >= 128:
            return
        w2p_emitted.add(j)
        ws = j % 12
        phj, kj = divmod(j, 64)
        S.add("gpsimd",
              (lambda kj=kj, ws=ws, phj=phj: lambda e:
               e.dma_start(
                   out=w2p_sb[:, ws * 1024:(ws + 1) * 1024],
                   in_=w2p[kj][:, phj * 1024:(phj + 1) * 1024]))(),
              waits=[("PE", w2p_used6[ws])] +
                    ([("sSt", w2p_all_st)]
                     if (blk_i == 0 and j == 0) else []),
              inc="gW", inc_amt=D16)
        w2p_fl[j] = ("gW", flush("gpsimd", "gW"))

    for blk_i in range(NBLK):
        w2p_emitted.clear()
        for dc in range(NDC):
            S.add("sync",
                  (lambda blk_i=blk_i, dc=dc: lambda sy:
                   sy.dma_start(
                       out=ln2_blk[:, dc * 512:(dc + 1) * 512],
                       in_=ln2_dram[blk_i, dc]))(),
                  waits=([("sSt", ln2_all_st), ("PE", ln2_used)]
                         if dc == 0 else []),
                  inc="sX", inc_amt=D16)
        ln2_ld = flush("sync", "sX")

        for hc in range(NHC):
            ws = hc % 3
            for wi, wsrc in ((0, w1Tt), (1, w3Tt)):
                S.add("gpsimd",
                      (lambda hc=hc, ws=ws, wi=wi, wsrc=wsrc: lambda gp:
                       gp.dma_start(
                           out=w13_sb[:, ws * 4096 + wi * 2048:
                                      ws * 4096 + wi * 2048 + 2048],
                           in_=wsrc[hc].rearrange("p dc f -> p (dc f)")))(),
                      waits=[("PE", w13_used[ws])] if wi == 0 else [],
                      inc="gW", inc_amt=D16)
            w13_ld[ws] = flush("gpsimd", "gW")
            gb = pU[2 * (hc % 2)]
            ub = pU[2 * (hc % 2) + 1]
            for wi in range(2):
                bank = gb if wi == 0 else ub
                for dc in range(NDC):
                    S.add("tensor",
                          (lambda bank=bank, ws=ws, wi=wi, dc=dc:
                           lambda t: t.matmul(
                               bank[:, 0:512],
                               w13_sb[:, ws * 4096 + wi * 2048 + dc * 128:
                                      ws * 4096 + wi * 2048 + (dc + 1) * 128],
                               ln2_blk[:, dc * 512:(dc + 1) * 512],
                               start=(dc == 0), stop=(dc == NDC - 1)))(),
                          waits=([("gW", w13_ld[ws]), ("sX", ln2_ld),
                                  ("DV", silu_free[hc % 2]),
                                  ("AC", ac_fo_prev),
                                  ("PE", h_used)] if (dc == 0 and wi == 0)
                                 else []),
                          inc="PE" if dc == NDC - 1 else None)
            w13_used[ws] = S.val("PE")
            pe_up = S.val("PE")
            sslot = hc % 2
            S.add("scalar",
                  (lambda gb=gb, sslot=sslot: lambda s: s.activation(
                      silu_sb[:, sslot * 512:(sslot + 1) * 512], gb[:, 0:512],
                      AF.Silu, bias=0.0, scale=1.0))(),
                  waits=[("PE", pe_up), ("DV", silu_free[sslot])], inc="AC")
            ac_si = S.val("AC")
            S.add("vector",
                  (lambda ub=ub, sslot=sslot, hc=hc: lambda v: v.tensor_tensor(
                      h_fm[:, hc * 512:(hc + 1) * 512],
                      silu_sb[:, sslot * 512:(sslot + 1) * 512],
                      ub[:, 0:512], ALU.mult))(),
                  waits=[("AC", ac_si)], inc="DV")
            silu_free[hc % 2] = S.val("DV")
        h_ready = S.val("DV")
        ln2_used = S.val("PE")

        for ph in range(2):
            for k in range(NHC):
                kk = ph * 64 + k
                for j in range(kk, kk + 8):
                    emit_w2p_load(j, blk_i)
                ws = kk % 12
                for tci in range(4):
                    for dd in range(2):
                        bank = pU[tci * 2 + dd]
                        w_mm = []
                        if tci == 0 and dd == 0:
                            w_mm.append(w2p_fl[min(kk + 4, 127)])
                            if kk == 0:
                                w_mm.append(("DV", h_ready))
                        if k == 0:
                            w_mm.append(("AC", fo_free[tci * 2 + dd]))
                        S.add("tensor",
                              (lambda bank=bank, k=k, ws=ws, tci=tci, dd=dd,
                                      ph=ph: lambda t: t.matmul(
                                  bank[:, 0:512],
                                  h_fm[:, k * 512 + tci * 128:k * 512 + (tci + 1) * 128],
                                  w2p_sb[:, ws * 1024 + dd * 512:
                                         ws * 1024 + (dd + 1) * 512],
                                  start=(k == 0), stop=(k == NHC - 1)))(),
                              waits=w_mm,
                              inc="PE" if (tci == 3 and dd == 1) else None)
                w2p_used6[ws] = S.val("PE")
            pe_dn = S.val("PE")
            for tci in range(4):
                for dd in range(2):
                    bank = pU[tci * 2 + dd]
                    dob = 2 * ph + dd
                    w_ev = [("PE", pe_dn)]
                    if ph == 0 and tci == 0 and dd == 0 and blk_i > 0:
                        w_ev += [("DV", x1_used[0]), ("DV", x1_used[1]),
                                 ("sSt", last_out)]
                    S.add("scalar",
                          (lambda bank=bank, tci=tci, dob=dob: lambda s: s.copy(
                              fo_sb[:, tci * D + dob * 512:tci * D + (dob + 1) * 512],
                              bank[:, 0:512]))(),
                          waits=w_ev, inc="AC")
                    fo_free[tci * 2 + dd] = S.val("AC")
        ac_fo = S.val("AC")
        ac_fo_prev = ac_fo
        h_used = S.val("PE")
        if "fo" in taps and blk_i == NBLK - 1:
            t_fo = tap("fo", [128, 4 * D])
            S.add("sync",
                  (lambda t_fo=t_fo: lambda sy: sy.dma_start(
                      out=t_fo, in_=fo_sb[:]))(),
                  waits=[("AC", ac_fo)], inc="sSt", inc_amt=D16)
            fo_tap_st = S.val("sSt")
        else:
            fo_tap_st = 0

        for tci in range(4):
            tc = blk_i * 4 + tci
            xslot = tci % 2
            S.add("sync",
                  (lambda tc=tc, xslot=xslot: lambda sy: sy.dma_start(
                      out=x1_t[:, xslot * D:(xslot + 1) * D],
                      in_=x1_dram[tc * 128:(tc + 1) * 128, :]))(),
                  waits=[("sSt", x1_all_st), ("DV", x1_used[xslot])],
                  inc="sX", inc_amt=D16)
            x1_ld = flush("sync", "sX")
            vst = emit_ln_stats(
                (lambda tci=tci: lambda j:
                 fo_sb[:, tci * D + 512 * j:tci * D + 512 * (j + 1)])(),
                [("AC", ac_fo), ("sSt", fo_tap_st)])
            S.add("vector",
                  (lambda tci=tci: lambda v: v.tensor_scalar(
                      fo_sb[:, tci * D:(tci + 1) * D],
                      fo_sb[:, tci * D:(tci + 1) * D],
                      ln_s1[:, 0:1], ln_s2[:, 0:1], ALU.add, ALU.mult))(),
                  waits=[("DV", vst)])
            S.add("vector",
                  (lambda tci=tci, xslot=xslot: lambda v: v.tensor_add(
                      fo_sb[:, tci * D:(tci + 1) * D],
                      fo_sb[:, tci * D:(tci + 1) * D],
                      x1_t[:, xslot * D:(xslot + 1) * D]))(),
                  waits=[("sX", x1_ld)], inc="DV")
            x1_used[xslot] = S.val("DV")
            v_out = S.val("DV")
            S.add("sync",
                  (lambda tc=tc, tci=tci: lambda sy: sy.dma_start(
                      out=out[tc * 128:(tc + 1) * 128, :],
                      in_=fo_sb[:, tci * D:(tci + 1) * D]))(),
                  waits=[("DV", v_out)], inc="sSt", inc_amt=D16)
            last_out = S.val("sSt")

    t_h = tap("hfm", [128, NHC * 512], BF16)
    if t_h is not None:
        S.add("sync",
              lambda sy: sy.dma_start(out=t_h, in_=h_fm[:]),
              waits=[("DV", S.val("DV")), ("AC", S.val("AC")),
                     ("PE", S.val("PE"))], inc="sSt", inc_amt=D16)
    emit_scratch_taps([("w2p", w2p, [NHC, 128, 2048], BF16),
                       ("y", y_dram, [NDC, NT, 128, 128], BF16),
                       ("x1", x1_dram, [L, D], F32),
                       ("ln2", ln2_dram, [NBLK, NDC, 128, 512], BF16)])

    S.add("sync", lambda sy: sy.nop(), waits=[("sSt", S.val("sSt"))])
    S.flush()
    ph5.close()
    stack.close()
    return nc, tap_h


_PROG_CACHE = {}


def _get_prog(taps=()):
    key = tuple(sorted(taps))
    if key not in _PROG_CACHE:
        _PROG_CACHE[key] = build_program(taps=key)
    return _PROG_CACHE[key]


def prep_inputs(inputs):
    x = np.ascontiguousarray(np.asarray(inputs["L0_attn_hidden_states"]))
    wq = np.asarray(inputs["L0_attn_Wq"])
    wk = np.asarray(inputs["L0_attn_Wk"])
    wv = np.asarray(inputs["L0_attn_Wv"])
    wo = np.asarray(inputs["L0_attn_Wo"])
    wfq = np.ascontiguousarray(np.asarray(inputs["L0_attn_Wfq"]))
    wfk = np.ascontiguousarray(np.asarray(inputs["L0_attn_Wfk"]))
    w1 = np.asarray(inputs["L0_ffn_W1"])
    w2 = np.asarray(inputs["L0_ffn_W2"])
    w3 = np.asarray(inputs["L0_ffn_W3"])

    def tile13(w):
        wt = w.T
        return np.ascontiguousarray(
            wt.reshape(NDC, 128, NHC, 128).transpose(2, 1, 0, 3))

    common = {
        "wqT": np.ascontiguousarray(wq.T),
        "wkT": np.ascontiguousarray(wk.T),
        "wvT": np.ascontiguousarray(wv.T),
        "woT": np.ascontiguousarray(wo.T),
        "wfq": wfq, "wfk": wfk,
        "w1Tt": tile13(w1),
        "w3Tt": tile13(w3),
        "w2tp": np.ascontiguousarray(
            w2.T.reshape(64, 128, D).transpose(1, 0, 2).reshape(FF, D)),
    }
    common["pad_in"] = np.zeros((4096, 1024), np.float32)
    in_maps = []
    for b in range(B):
        m = dict(common)
        m["x_in"] = np.ascontiguousarray(x[b])
        in_maps.append(m)
    return in_maps


def kernel(**inputs):
    nc, _ = _get_prog()
    in_maps = prep_inputs(inputs)
    res = run_bass_kernel_spmd(nc, in_maps, core_ids=list(range(B)))
    return np.stack([np.asarray(res.results[b]["out"]) for b in range(B)], axis=0)


# version 11 (final)
# speedup vs baseline: 1.5248x; 1.0358x over previous
import contextlib
import math

import numpy as np

import concourse.bass as bass
import concourse.mybir as mybir
from concourse.bass_utils import run_bass_kernel_spmd

F32 = mybir.dt.float32
BF16 = mybir.dt.bfloat16
I32 = mybir.dt.int32
AF = mybir.ActivationFunctionType
ALU = mybir.AluOpType

B = 8
L = 2048
D = 2048
H = 32
FD = 64
FF = 8192
NT = L // 128
NDC = D // 128
NSC = 4
NG = 8
NBLK = 4
NHC = FF // 128
LN_EPS = 1e-5
DEN_EPS = 1e-12
SOFF = [0, 512, 896, 1152]
SLEN = 1280


class Sched:

    def __init__(self, nc, sems):
        self.nc = nc
        self.sems = sems
        self.count = {k: 0 for k in sems}
        self.steps = []

    def val(self, sem):
        return self.count[sem]

    log = []

    def add(self, engine, fn, waits=(), inc=None, inc_amt=1):
        w = [(s, v) for (s, v) in waits if v is not None and v > 0]
        if inc is not None:
            self.count[inc] += inc_amt
        self.steps.append((engine, fn, w, inc, inc_amt))
        Sched.log.append((engine, tuple(w), inc, inc_amt))
        return self.count[inc] if inc is not None else None

    def flush(self):
        with self.nc.Block() as blk:
            for name in ("sync", "tensor", "vector", "scalar", "gpsimd"):
                if not any(s[0] == name for s in self.steps):
                    continue

                def make(nm):
                    def body(eng):
                        for (en, fn, waits, inc, amt) in self.steps:
                            if en != nm:
                                continue
                            for (s, v) in waits:
                                eng.wait_ge(self.sems[s], v)
                            ins = fn(eng)
                            if inc is not None:
                                ins.then_inc(self.sems[inc], amt)
                    return body
                getattr(blk, name)(make(name))
        self.steps = []


D16 = 16


def build_program(taps=(), stop_after=5, ngroups=NG, attn_sub='all'):
    nc = bass.Bass("TRN2", target_bir_lowering=False)

    w2tp = nc.dram_tensor("w2tp", [FF, D], F32, kind="ExternalInput").ap()
    wfq = nc.dram_tensor("wfq", [H, 64, FD], F32, kind="ExternalInput").ap()
    wfk = nc.dram_tensor("wfk", [H, 64, FD], F32, kind="ExternalInput").ap()
    x_in = nc.dram_tensor("x_in", [L, D], F32, kind="ExternalInput").ap()
    wkT = nc.dram_tensor("wkT", [D, D], F32, kind="ExternalInput").ap()
    wqT = nc.dram_tensor("wqT", [D, D], F32, kind="ExternalInput").ap()
    wvT = nc.dram_tensor("wvT", [D, D], F32, kind="ExternalInput").ap()
    woT = nc.dram_tensor("woT", [D, D], F32, kind="ExternalInput").ap()
    w1Tt = nc.dram_tensor("w1Tt", [NHC, 128, NDC, 128], F32, kind="ExternalInput").ap()
    w3Tt = nc.dram_tensor("w3Tt", [NHC, 128, NDC, 128], F32, kind="ExternalInput").ap()
    pad_in = nc.dram_tensor("pad_in", [4096, 1024], F32, kind="ExternalInput").ap()
    out = nc.dram_tensor("out", [L, D], F32, kind="ExternalOutput").ap()

    u_dram = nc.dram_tensor("u_scr", [128, 64, 2048], BF16, kind="Internal").ap()
    fl_dram = nc.dram_tensor("fl_scr", [128, 16], F32, kind="Internal").ap()
    w2p = nc.dram_tensor("w2p_scr", [NHC, 128, 2048], BF16, kind="Internal").ap()
    y_dram = nc.dram_tensor("y_scr", [NDC, NT, 128, 128], BF16, kind="Internal").ap()
    x1_dram = nc.dram_tensor("x1_scr", [L, D], F32, kind="Internal").ap()
    ln2_dram = nc.dram_tensor("ln2_scr", [NBLK, NDC, 128, 512], BF16, kind="Internal").ap()

    tap_h = {}

    def tap(name, shape, dtype=F32):
        if name in taps:
            tap_h[name] = nc.dram_tensor("tap_" + name, shape, dtype,
                                         kind="ExternalOutput").ap()
            return tap_h[name]
        return None

    stack = contextlib.ExitStack()
    sems = {n: stack.enter_context(nc.semaphore("sem_" + n))
            for n in ["gW", "sX", "sSt", "PE", "DV", "AC", "GP"]}
    S = Sched(nc, sems)
    _tapped = set()

    def emit_scratch_taps(avail):
        for name, src_ap, shape, dt in avail:
            if name in _tapped:
                continue
            t = tap(name, shape, dt)
            if t is not None:
                _tapped.add(name)
                S.add("sync",
                      (lambda t=t, src_ap=src_ap: lambda sy: sy.dma_start(
                          out=t, in_=src_ap))(),
                      waits=[("sSt", S.val("sSt"))], inc="sSt", inc_amt=D16)

    def sb(st, name, shape, dt=F32):
        return st.enter_context(nc.sbuf_tensor(name, shape, dt))

    def psum(st, name, dt=F32):
        return st.enter_context(nc.psum_tensor(name, [128, 512], dt))

    ident = sb(stack, "ident", [128, 128], BF16)
    mask_tri = sb(stack, "mask_tri", [128, 128], BF16)
    h128 = sb(stack, "h128", [128, 128], BF16)
    h64bd = sb(stack, "h64bd", [128, 128], BF16)
    wfq_bd = sb(stack, "wfq_bd", [128, 16 * 128], BF16)
    wfk_bd = sb(stack, "wfk_bd", [128, 16 * 128], BF16)
    x0_t = sb(stack, "x0_t", [128, D * 2], F32)
    lnst = sb(stack, "lnst", [128, 48], F32)
    lnmv = sb(stack, "lnmv", [128, 4], F32)
    ln_s1 = sb(stack, "ln_s1", [128, 2], F32)
    ln_s2 = sb(stack, "ln_s2", [128, 2], F32)
    eps_sb = sb(stack, "eps_sb", [128, 1], F32)
    flsb = sb(stack, "flsb", [128, 16], F32)

    def flush(queue, sem):
        S.add(queue, lambda e: e.dma_start(out=flsb[:], in_=fl_dram),
              inc=sem, inc_amt=D16)
        return S.val(sem)

    ph01 = contextlib.ExitStack()
    iop = sb(ph01, "iop", [128, 128], I32)
    iof = sb(ph01, "iof", [128, 128], I32)
    it1 = sb(ph01, "it1", [128, 128], I32)
    it2 = sb(ph01, "it2", [128, 128], I32)
    foldi = sb(ph01, "foldi", [128, 2048 * 4], BF16)
    foldo = sb(ph01, "foldo", [128, 2048 * 3], BF16)
    pbank = [psum(ph01, f"pf{i}") for i in range(4)]

    S.add("gpsimd", lambda g: g.iota(iop[:], pattern=[[0, 128]], base=0,
                                     channel_multiplier=1), inc="GP")
    S.add("gpsimd", lambda g: g.iota(iof[:], pattern=[[1, 128]], base=0,
                                     channel_multiplier=0), inc="GP")
    v_io = S.val("GP")
    S.add("vector", lambda v: v.tensor_tensor(ident[:], iop[:], iof[:], ALU.is_equal),
          waits=[("GP", v_io)])
    S.add("vector", lambda v: v.tensor_tensor(mask_tri[:], iof[:], iop[:], ALU.is_ge))
    S.add("vector", lambda v: v.tensor_tensor(it1[:], iop[:], iof[:], ALU.bitwise_and))
    for sh in (4, 2, 1):
        S.add("vector", (lambda sh=sh: lambda v: v.tensor_scalar(
            it2[:], it1[:], sh, None, ALU.arith_shift_right))())
        S.add("vector", lambda v: v.tensor_tensor(it1[:], it1[:], it2[:],
                                                  ALU.bitwise_xor))
    S.add("vector", lambda v: v.tensor_scalar(it2[:], it1[:], 1, None, ALU.bitwise_and))
    S.add("vector", lambda v: v.tensor_scalar(h128[:], it2[:], -2.0, 1.0,
                                              ALU.mult, ALU.add), inc="DV")
    v_h128 = S.val("DV")
    S.add("vector", lambda v: v.memset(h64bd[:], 0.0), inc="DV")
    v_memset = S.val("DV")
    S.add("vector", lambda v: v.tensor_copy(h64bd[0:64, 0:64], h128[0:64, 0:64]),
          waits=[("DV", v_h128), ("DV", v_memset)])
    S.add("vector", lambda v: v.tensor_copy(h64bd[64:128, 64:128], h128[0:64, 0:64]))
    S.add("vector", lambda v: v.memset(eps_sb[:], LN_EPS))
    S.add("vector", lambda v: v.memset(wfq_bd[:], 0.0))
    S.add("vector", lambda v: v.memset(wfk_bd[:], 0.0), inc="DV")
    v_const = S.val("DV")
    for pg in range(16):
        for (dst, srcw) in ((wfq_bd, wfq), (wfk_bd, wfk)):
            S.add("gpsimd",
                  (lambda dst=dst, srcw=srcw, pg=pg: lambda g: g.dma_start(
                      out=dst[0:64, pg * 128:pg * 128 + 64], in_=srcw[2 * pg]))(),
                  waits=[("DV", v_const)] if pg == 0 else [],
                  inc="gW", inc_amt=D16)
            S.add("gpsimd",
                  (lambda dst=dst, srcw=srcw, pg=pg: lambda g: g.dma_start(
                      out=dst[64:128, pg * 128 + 64:pg * 128 + 128],
                      in_=srcw[2 * pg + 1]))(),
                  inc="gW", inc_amt=D16)
    g_wf = flush("gpsimd", "gW")
    dv_wf = S.val("DV")

    fA_fl = {}
    fA_emitted = set()
    slot_mm = [0, 0, 0, 0]
    ev_done = [0, 0, 0, 0]
    st_done = [0, 0, 0]

    def emit_foldA_load(bp):
        if bp in fA_emitted or bp >= 64:
            return
        fA_emitted.add(bp)
        slot = bp % 4
        S.add("gpsimd",
              (lambda bp=bp, slot=slot: lambda g: g.dma_start(
                  out=foldi[:, slot * 2048:(slot + 1) * 2048],
                  in_=w2tp[2 * bp * 64:2 * bp * 64 + 128, :]))(),
              waits=[("PE", slot_mm[slot])], inc="gW", inc_amt=D16)
        fA_fl[bp] = flush("gpsimd", "gW")

    for bp in range(64):
        for _j in range(bp, bp + 3):
            emit_foldA_load(_j)
        slot = bp % 4
        sloto = bp % 3
        for dob in range(4):
            bank = pbank[dob]
            S.add("tensor",
                  (lambda bank=bank, slot=slot, dob=dob: lambda t: t.matmul(
                      bank[:, 0:512], h64bd[:],
                      foldi[:, slot * 2048 + dob * 512:slot * 2048 + (dob + 1) * 512],
                      start=True, stop=True))(),
                  waits=[("gW", fA_fl[min(bp + 2, 63)]), ("DV", v_const),
                         ("DV", ev_done[dob])],
                  inc="PE")
            mmc = S.val("PE")
            S.add("vector",
                  (lambda bank=bank, sloto=sloto, dob=dob: lambda v: v.tensor_copy(
                      foldo[:, sloto * 2048 + dob * 512:sloto * 2048 + (dob + 1) * 512],
                      bank[:, 0:512]))(),
                  waits=[("PE", mmc), ("sSt", st_done[sloto])], inc="DV")
            ev_done[dob] = S.val("DV")
        slot_mm[slot] = S.val("PE")
        ev4 = S.val("DV")
        S.add("sync",
              (lambda bp=bp, sloto=sloto: lambda sy: sy.dma_start(
                  out=u_dram[2 * bp],
                  in_=foldo[0:64, sloto * 2048:(sloto + 1) * 2048]))(),
              waits=[("DV", ev4)], inc="sSt", inc_amt=D16)
        S.add("sync",
              (lambda bp=bp, sloto=sloto: lambda sy: sy.dma_start(
                  out=u_dram[2 * bp + 1],
                  in_=foldo[64:128, sloto * 2048:(sloto + 1) * 2048]))(),
              inc="sSt", inc_amt=D16)
        S.add("sync", lambda sy: sy.dma_start(out=fl_dram, in_=flsb[:]),
              inc="sSt", inc_amt=D16)
        st_done[sloto] = S.val("sSt")
    stA = S.val("sSt")

    scaleW = 1.0 / math.sqrt(FF)
    fB_fl = {}
    fB_emitted = set()
    slot_mm = [0, 0, 0, 0]
    ev_done = [0, 0, 0, 0]
    st_done = [0, 0, 0]

    def emit_foldB_load(ai):
        if ai in fB_emitted or ai >= 64:
            return
        fB_emitted.add(ai)
        slot = ai % 4
        S.add("gpsimd",
              (lambda ai=ai, slot=slot: lambda g: g.dma_start(
                  out=foldi[:, slot * 2048:(slot + 1) * 2048],
                  in_=u_dram[:, ai, :]))(),
              waits=[("PE", slot_mm[slot])] +
                    ([("sSt", stA)] if ai == 0 else []),
              inc="gW", inc_amt=D16)
        fB_fl[ai] = flush("gpsimd", "gW")

    for ai in range(64):
        for _j in range(ai, ai + 3):
            emit_foldB_load(_j)
        slot = ai % 4
        sloto = ai % 3
        for dob in range(4):
            bank = pbank[dob]
            S.add("tensor",
                  (lambda bank=bank, slot=slot, dob=dob: lambda t: t.matmul(
                      bank[:, 0:512], h128[:],
                      foldi[:, slot * 2048 + dob * 512:slot * 2048 + (dob + 1) * 512],
                      start=True, stop=True))(),
                  waits=[("gW", fB_fl[min(ai + 2, 63)]), ("DV", ev_done[dob])],
                  inc="PE")
            mmc = S.val("PE")
            S.add("vector",
                  (lambda bank=bank, sloto=sloto, dob=dob: lambda v: v.tensor_scalar(
                      foldo[:, sloto * 2048 + dob * 512:sloto * 2048 + (dob + 1) * 512],
                      bank[:, 0:512], scaleW, None, ALU.mult))(),
                  waits=[("PE", mmc), ("sSt", st_done[sloto])], inc="DV")
            ev_done[dob] = S.val("DV")
        slot_mm[slot] = S.val("PE")
        ev4 = S.val("DV")
        S.add("sync",
              (lambda ai=ai, sloto=sloto: lambda sy: sy.dma_start(
                  out=w2p[ai],
                  in_=foldo[:, sloto * 2048:(sloto + 1) * 2048]))(),
              waits=[("DV", ev4)], inc="sSt", inc_amt=D16)
        S.add("sync", lambda sy: sy.dma_start(out=fl_dram, in_=flsb[:]),
              inc="sSt", inc_amt=D16)
        st_done[sloto] = S.val("sSt")
    w2p_all_st = S.val("sSt")
    S.flush()
    ph01.close()

    if stop_after <= 1:
        emit_scratch_taps([("w2p", w2p, [NHC, 128, 2048], BF16),
                           ("u", u_dram, [128, 64, 2048], BF16)])
        S.add("sync", lambda sy: sy.nop(),
              waits=[("sSt", S.val("sSt")), ("sX", S.val("sX")),
                     ("gW", S.val("gW"))])
        S.flush()
        stack.close()
        return nc, tap_h

    ph23 = contextlib.ExitStack()
    ln1_fm = sb(ph23, "ln1_fm", [128, NDC * L], BF16)
    ln1_tm = sb(ph23, "ln1_tm", [128, D * 2], BF16)
    wq_g = sb(ph23, "wq_g", [128, NDC * 256], BF16)
    wk_g = sb(ph23, "wk_g", [128, NDC * 256], BF16)
    wv_g = sb(ph23, "wv_g", [128, NDC * 256], BF16)
    qk_sb = sb(ph23, "qk_sb", [128, 512 * 2], BF16)
    eq_t = sb(ph23, "eq_t", [128, 128 * 4], BF16)
    ek_strip = sb(ph23, "ek_strip", [128, 2 * L], BF16)
    fq_fm = sb(ph23, "fq_fm", [128, 4 * L], BF16)
    fk_fm = sb(ph23, "fk_fm", [128, 4 * L], BF16)
    v65 = sb(ph23, "v65", [128, NT * 260], BF16)
    rk = sb(ph23, "rk", [128, NT * 4], F32)
    ksum = sb(ph23, "ksum", [128, 4], F32)
    s_sb = sb(ph23, "s_sb", [128, 4 * SLEN], BF16)
    st_f32 = sb(ph23, "st_f32", [128, 130], F32)
    st_pad = sb(ph23, "st_pad", [128, 260], BF16)
    y_tm = sb(ph23, "y_tm", [128, 128 * 2], BF16)
    y_strip = sb(ph23, "y_strip", [128, 2 * 512], BF16)
    den_t = sb(ph23, "den_t", [128, 4], F32)
    rec_t = sb(ph23, "rec_t", [128, 4], F32)
    pA = psum(ph23, "pA")
    pB = psum(ph23, "pB")
    pC = psum(ph23, "pC")
    pD = psum(ph23, "pD")
    pE = psum(ph23, "pE")
    pF = psum(ph23, "pF")
    tpP = [psum(ph23, "tp0", BF16), psum(ph23, "tp1", BF16)]

    ln_sqrt_ac = [0, 0]

    def ln_stats(src_fn, first_waits, par=0):
        for j in range(4):
            S.add("vector",
                  (lambda j=j, par=par: lambda v: v.bn_stats(
                      lnst[:, 24 * par + 6 * j:24 * par + 6 * j + 6],
                      src_fn(j)))(),
                  waits=(list(first_waits) + [("AC", ln_sqrt_ac[par])])
                  if j == 0 else [], inc="DV")
        v4 = S.val("DV")
        S.add("vector",
              (lambda par=par: lambda v: v.bn_aggr(
                  lnmv[:, 2 * par:2 * par + 2],
                  lnst[:, 24 * par:24 * par + 24]))(),
              waits=[("DV", v4)], inc="DV")
        return S.val("DV")

    def ln_finish(par, va):
        S.add("scalar",
              (lambda par=par: lambda s: s.activation(
                  ln_s2[:, par:par + 1], lnmv[:, 2 * par + 1:2 * par + 2],
                  AF.Sqrt, bias=eps_sb[:], scale=1.0))(),
              waits=[("DV", va)], inc="AC")
        ln_sqrt_ac[par] = S.val("AC")
        vs = S.val("AC")
        S.add("vector",
              (lambda par=par: lambda v: v.reciprocal(
                  ln_s2[:, par:par + 1], ln_s2[:, par:par + 1]))(),
              waits=[("AC", vs)], inc="DV")
        vr = S.val("DV")
        S.add("vector",
              (lambda par=par: lambda v: v.tensor_scalar(
                  ln_s1[:, par:par + 1], lnmv[:, 2 * par:2 * par + 1], -1.0,
                  None, ALU.mult))(),
              waits=[("DV", vr)], inc="DV")
        return S.val("DV")

    def emit_ln_stats(src_fn, first_waits, par=0):
        va = ln_stats(src_fn, first_waits, par)
        return ln_finish(par, va)

    tp_free = [0, 0]
    apply_done = [0, 0]
    ln1tm_free = [0, 0]
    stage2 = {}
    for tc in range(NT + 1):
        if tc < NT:
            slot = tc % 2
            S.add("sync",
                  (lambda tc=tc, slot=slot: lambda sy: sy.dma_start(
                      out=x0_t[:, slot * D:(slot + 1) * D],
                      in_=x_in[tc * 128:(tc + 1) * 128, :]))(),
                  waits=[("DV", apply_done[slot])], inc="sX", inc_amt=D16)
            x_ld = flush("sync", "sX")
            va = ln_stats(
                (lambda slot=slot: lambda j:
                 x0_t[:, slot * D + 512 * j:slot * D + 512 * (j + 1)])(),
                [("sX", x_ld)], par=slot)
            stage2[tc] = va
        if tc >= 1:
            tp_ = tc - 1
            slot = tp_ % 2
            vstats = ln_finish(slot, stage2[tp_])
            S.add("vector",
                  (lambda slot=slot: lambda v: v.tensor_scalar(
                      ln1_tm[:, slot * D:(slot + 1) * D],
                      x0_t[:, slot * D:(slot + 1) * D],
                      ln_s1[:, slot:slot + 1], ln_s2[:, slot:slot + 1],
                      ALU.add, ALU.mult))(),
                  waits=[("DV", vstats), ("PE", ln1tm_free[slot])], inc="DV")
            apply_done[slot] = S.val("DV")
            for q4 in range(4):
                hp = q4 % 2
                for j in range(4):
                    dc = q4 * 4 + j
                    S.add("tensor",
                          (lambda slot=slot, dc=dc, hp=hp, j=j: lambda t: t.transpose(
                              tpP[hp][:, j * 128:(j + 1) * 128],
                              ln1_tm[:, slot * D + dc * 128:slot * D + (dc + 1) * 128],
                              ident[:]))(),
                          waits=([("DV", apply_done[slot]), ("DV", tp_free[hp])]
                                 if j == 0 else []),
                          inc="PE" if j == 3 else None)
                pe_tp = S.val("PE")
                S.add("vector",
                      (lambda tc2=tp_, q4=q4, hp=hp: lambda v: v.tensor_copy(
                          ln1_fm.ap().rearrange("p (dc t) -> p dc t", t=L)[
                              :, q4 * 4:(q4 + 1) * 4, tc2 * 128:(tc2 + 1) * 128],
                          tpP[hp][:, 0:512].rearrange(
                              "p (dc t) -> p dc t", t=128)))(),
                      waits=[("PE", pe_tp)], inc="DV")
                tp_free[hp] = S.val("DV")
            ln1tm_free[slot] = S.val("PE")
        ln1_done = S.val("DV")

    t_ln1 = tap("ln1_fm", [128, NDC * L], BF16)
    if t_ln1 is not None:
        S.add("sync", lambda sy: sy.dma_start(out=t_ln1, in_=ln1_fm[:]),
              waits=[("DV", ln1_done)], inc="sSt", inc_amt=D16)

    if stop_after <= 2:
        S.add("sync", lambda sy: sy.nop(),
              waits=[("sSt", S.val("sSt")), ("sX", S.val("sX")),
                     ("gW", S.val("gW")), ("DV", ln1_done)])
        S.flush()
        ph23.close()
        stack.close()
        return nc, tap_h

    S.add("vector", lambda v: v.memset(fq_fm[:], 0.0))
    S.add("vector", lambda v: v.memset(fk_fm[:], 0.0), inc="DV")
    strips_z = S.val("DV")
    prev_group_pe = 0
    prev_proj_pe = 0
    ystrip_st = [0, 0]
    for g in range(ngroups):
        wld = {}
        for (wbuf, wsrc) in ((wk_g, wkT), (wq_g, wqT), (wv_g, wvT)):
            for dc in range(NDC):
                S.add("gpsimd",
                      (lambda wbuf=wbuf, wsrc=wsrc, dc=dc, g=g: lambda gp:
                       gp.dma_start(
                           out=wbuf[:, dc * 256:(dc + 1) * 256],
                           in_=wsrc[dc * 128:(dc + 1) * 128,
                                    g * 256:(g + 1) * 256]))(),
                      waits=[("PE", prev_proj_pe)] if dc == 0 else [],
                      inc="gW", inc_amt=D16)
            wld[id(wbuf)] = flush("gpsimd", "gW")

        qkbank = [pA, pB]
        zbank = [pC, pD]
        qk_free = [0, 0]
        z_free = [0, 0]
        tsc_tp_pe = {}

        def proj_feat(which, wbuf, wfbd, dst_fm, g=g, wld=wld,
                      tsc_tp_pe=tsc_tp_pe, qk_free=qk_free, z_free=z_free):
            pending = []

            def zchain(p, tsc, slot, ac_ev, u):
                            exp_acs = []
                            prev_tp = tsc_tp_pe.get((which, p, tsc - 1), 0)
                            if which == "q":
                                prev_tp = max(prev_tp, tsc_tp_pe.get(("q", p - 1, 3), 0))
                            else:
                                prev_tp = max(prev_tp, tsc_tp_pe.get(("k", p - 1, 3), 0))
                            for tci in range(4):
                                tc = tsc * 4 + tci
                                zb = zbank[tci % 2]
                                S.add("tensor",
                                      (lambda zb=zb, slot=slot, tci=tci, wfbd=wfbd,
                                              g=g, p=p: lambda t: t.matmul(
                                          zb[:, 0:128],
                                          qk_sb[:, slot * 512 + tci * 128:
                                                slot * 512 + (tci + 1) * 128],
                                          wfbd[:, (2 * g + p) * 128:(2 * g + p + 1) * 128],
                                          start=True, stop=True))(),
                                      waits=[("AC", ac_ev), ("gW", g_wf),
                                             ("AC", z_free[tci % 2])],
                                      inc="PE")
                                pe_z = S.val("PE")
                                if which == "k":
                                    dst = (lambda p=p, tc=tc:
                                           ek_strip[:, p * L + tc * 128:p * L + (tc + 1) * 128])
                                else:
                                    dst = (lambda tci=tci:
                                           eq_t[:, tci * 128:(tci + 1) * 128])
                                S.add("scalar",
                                      (lambda dst=dst, zb=zb: lambda s: s.activation(
                                          dst(), zb[:, 0:128], AF.Exp, bias=0.0,
                                          scale=1.0))(),
                                      waits=[("PE", pe_z)] +
                                            ([("PE", prev_tp)] if (which == "q" and tci == 0)
                                             else []),
                                      inc="AC")
                                z_free[tci % 2] = S.val("AC")
                                exp_acs.append(S.val("AC"))
                                if which == "k":
                                    S.add("vector",
                                          (lambda p=p, tc=tc: lambda v: v.reduce_sum(
                                              ksum[:, 0:2],
                                              ek_strip[:, p * L + tc * 128:
                                                       p * L + (tc + 1) * 128]
                                              .rearrange("p (hh f) -> p hh f", hh=2),
                                              axis=mybir.AxisListType.X))(),
                                          waits=[("AC", S.val("AC"))], inc="DV")
                                    vks = S.val("DV")
                                    S.add("vector",
                                          (lambda p=p, tc=tc: lambda v: v.reciprocal(
                                              rk[:, tc * 4 + 2 * p:tc * 4 + 2 * p + 2],
                                              ksum[:, 0:2]))(),
                                          waits=[("DV", vks)], inc="DV")
                            tph = u % 2
                            for tci in range(4):
                                tc = tsc * 4 + tci
                                if which == "k":
                                    src = (lambda p=p, tc=tc:
                                           ek_strip[:, p * L + tc * 128:p * L + (tc + 1) * 128])
                                else:
                                    src = (lambda tci=tci:
                                           eq_t[:, tci * 128:(tci + 1) * 128])
                                S.add("tensor",
                                      (lambda src=src, tph=tph, tci=tci: lambda t:
                                       t.transpose(
                                           tpP[tph][:, tci * 128:(tci + 1) * 128],
                                           src(), ident[:]))(),
                                      waits=([("AC", exp_acs[3]), ("DV", tp_free[tph])]
                                             if tci == 0 else []),
                                      inc="PE" if tci == 3 else None)
                            pe_tp4 = S.val("PE")
                            tsc_tp_pe[(which, p, tsc)] = pe_tp4
                            for par in range(2):
                                hstrip = 2 * p + par
                                S.add("vector",
                                      (lambda dst_fm=dst_fm, hstrip=hstrip, par=par,
                                              tsc=tsc, tph=tph: lambda v: v.tensor_copy(
                                          dst_fm[64 * par:64 * par + 64,
                                                 hstrip * L + tsc * 512:
                                                 hstrip * L + (tsc + 1) * 512],
                                          tpP[tph][64 * par:64 * par + 64, 0:512]))(),
                                      waits=[("PE", pe_tp4), ("DV", strips_z)]
                                      if par == 0 else [],
                                      inc="DV" if par == 1 else None)
                            tp_free[tph] = S.val("DV")
            for p in range(2):
                for tsc in range(4):
                    u = p * 4 + tsc
                    slot = u % 2
                    bank = qkbank[slot]
                    for dc in range(NDC):
                        S.add("tensor",
                              (lambda bank=bank, wbuf=wbuf, dc=dc, p=p, tsc=tsc:
                               lambda t: t.matmul(
                                   bank[:, 0:512],
                                   wbuf[:, dc * 256 + p * 128:dc * 256 + (p + 1) * 128],
                                   ln1_fm[:, dc * L + tsc * 512:dc * L + (tsc + 1) * 512],
                                   start=(dc == 0), stop=(dc == NDC - 1)))(),
                              waits=([("gW", wld[id(wbuf)]), ("DV", ln1_done),
                                      ("AC", qk_free[slot])] if dc == 0 else []),
                              inc="PE" if dc == NDC - 1 else None)
                    pe_p = S.val("PE")
                    S.add("scalar",
                          (lambda bank=bank, slot=slot: lambda s: s.copy(
                              qk_sb[:, slot * 512:(slot + 1) * 512],
                              bank[:, 0:512]))(),
                          waits=[("PE", pe_p)], inc="AC")
                    ac_ev = S.val("AC")
                    qk_free[slot] = ac_ev
                    pending.append((p, tsc, slot, ac_ev, u))
                    if len(pending) > 1:
                        zchain(*pending.pop(0))

            while pending:
                zchain(*pending.pop(0))
            return

        proj_feat("k", wk_g, wfk_bd, fk_fm)
        k_done = S.val("DV")
        if attn_sub == "k":
            break
        proj_feat("q", wq_g, wfq_bd, fq_fm)
        q_done = S.val("DV")
        q_ac = S.val("AC")
        if attn_sub == "kq":
            break

        v_free = [0, 0]
        for tc in range(NT):
            slot = tc % 2
            bank = qkbank[slot]
            for dc in range(NDC):
                S.add("tensor",
                      (lambda bank=bank, dc=dc, tc=tc, g=g: lambda t: t.matmul(
                          bank[:, 0:256],
                          ln1_fm[:, dc * L + tc * 128:dc * L + (tc + 1) * 128],
                          wv_g[:, dc * 256:(dc + 1) * 256],
                          start=(dc == 0), stop=(dc == NDC - 1)))(),
                      waits=([("gW", wld[id(wv_g)]), ("DV", k_done),
                              ("AC", q_ac),
                              ("DV", v_free[slot])] if dc == 0 else []),
                      inc="PE" if dc == NDC - 1 else None)
            pe_v = S.val("PE")
            for hh in range(4):
                S.add("vector",
                      (lambda bank=bank, tc=tc, hh=hh: lambda v: v.tensor_scalar(
                          v65[:, tc * 260 + hh * 65:tc * 260 + hh * 65 + 64],
                          bank[:, hh * 64:(hh + 1) * 64],
                          rk[:, tc * 4 + hh:tc * 4 + hh + 1], None, ALU.mult))(),
                      waits=[("PE", pe_v)] if hh == 0 else [])
            S.add("vector",
                  (lambda tc=tc: lambda v: v.tensor_copy(
                      v65.ap().rearrange("p (t hh c) -> p t hh c", t=NT, hh=4)[
                          :, tc, :, 64:65].rearrange("p hh c -> p (hh c)"),
                      rk[:, tc * 4:(tc + 1) * 4]))(),
                  inc="DV")
            v_free[slot] = S.val("DV")
        v_done = S.val("DV")
        prev_proj_pe = S.val("PE")
        if attn_sub == "kqv":
            break

        S.add("vector", lambda v: v.memset(st_f32[:], 0.0),
              waits=[("PE", prev_group_pe)])
        S.add("vector", lambda v: v.memset(st_pad[:], 0.0), inc="DV")
        st_ready = S.val("DV")

        sbank = [pA, pB]
        ybank = [pC, pD]
        stbank = pE
        s_ev = {}
        prev_y_pe = 0
        ytm_free_pe = 0
        ybank_free_dv = 0
        for sc in range(NSC):
            sb_free = s_ev.get("free", [(0, 0), (0, 0)])
            it_s = 0
            for hh in range(4):
                p, par = hh // 2, hh % 2
                for j in range(4):
                    bslot_s = it_s % 2
                    bank = sbank[bslot_s]
                    tk = sc * 4 + j
                    nj = 512 - j * 128
                    w0 = [("AC", sb_free[bslot_s][0]), ("DV", sb_free[bslot_s][1])]
                    if it_s == 0:
                        w0 += [("DV", v_done), ("PE", prev_y_pe)]
                    S.add("tensor",
                          (lambda bank=bank, hh=hh, tk=tk, j=j, sc=sc,
                                  nj=nj: lambda t: t.matmul(
                              bank[:, 0:nj],
                              fk_fm[:, hh * L + tk * 128:hh * L + (tk + 1) * 128],
                              fq_fm[:, hh * L + sc * 512 + j * 128:
                                    hh * L + (sc + 1) * 512],
                              start=True, stop=True))(),
                          waits=w0, inc="PE")
                    pe_s = S.val("PE")
                    base = hh * SLEN + SOFF[j]
                    S.add("vector",
                          (lambda bank=bank, base=base: lambda v: v.tensor_tensor(
                              s_sb[:, base:base + 128], bank[:, 0:128],
                              mask_tri[:], ALU.mult))(),
                          waits=[("PE", pe_s)],
                          inc=None if nj > 128 else "DV")
                    if nj > 128:
                        S.add("vector",
                              (lambda bank=bank, base=base, nj=nj: lambda v:
                               v.tensor_copy(s_sb[:, base + 128:base + nj],
                                             bank[:, 128:nj]))(),
                              inc="DV")
                    sb_free[bslot_s] = (S.val("AC"), S.val("DV"))
                    it_s += 1
            s_ev["free"] = sb_free
            s_ac, s_dv = S.val("AC"), S.val("DV")
            if attn_sub == "s":
                prev_y_pe = S.val("PE")
                continue

            for i in range(4):
                tq = sc * 4 + i
                for hh in range(4):
                    p, par = hh // 2, hh % 2
                    bank = ybank[par]
                    S.add("tensor",
                          (lambda bank=bank, p=p, hh=hh, tq=tq, st=(hh < 2):
                           lambda t: t.matmul(
                               bank[:, 65 * p:65 * p + 65],
                               fq_fm[:, hh * L + tq * 128:hh * L + (tq + 1) * 128],
                               st_pad[:, hh * 65:hh * 65 + 65],
                               start=st, stop=False))(),
                          waits=([("DV", st_ready), ("AC", s_ac), ("DV", s_dv),
                                  ("DV", ybank_free_dv)] if hh == 0 else []))
                for hh in range(4):
                    p, par = hh // 2, hh % 2
                    bank = ybank[par]
                    for j in range(i + 1):
                        off = hh * SLEN + SOFF[j] + (i - j) * 128
                        last = (j == i) and (hh >= 2)
                        S.add("tensor",
                              (lambda bank=bank, p=p, off=off, hh=hh, sc=sc,
                                      j=j, last=last: lambda t: t.matmul(
                                  bank[:, 65 * p:65 * p + 65],
                                  s_sb[:, off:off + 128],
                                  v65[:, (sc * 4 + j) * 260 + hh * 65:
                                      (sc * 4 + j) * 260 + hh * 65 + 65],
                                  start=False, stop=last))(),
                              inc="PE" if (j == i and hh == 3) else None)
                pe_y = S.val("PE")
                prev_y_pe = pe_y
                S.add("vector",
                      lambda v: v.tensor_scalar(den_t[:, 0:2], pC[:, 64:130:65],
                                                DEN_EPS, None, ALU.add),
                      waits=[("PE", pe_y)])
                S.add("vector",
                      lambda v: v.tensor_scalar(den_t[:, 2:4], pD[:, 64:130:65],
                                                DEN_EPS, None, ALU.add),
                      inc="DV")
                v_den = S.val("DV")
                S.add("vector", lambda v: v.reciprocal(rec_t[:], den_t[:]),
                      waits=[("DV", v_den)], inc="DV")
                v_rec = S.val("DV")
                for hh in range(4):
                    p, par = hh // 2, hh % 2
                    bank = ybank[par]
                    S.add("vector",
                          (lambda bank=bank, p=p, par=par, ridx=2 * par + p:
                           lambda v: v.tensor_scalar(
                               y_tm[:, p * 128 + par * 64:p * 128 + par * 64 + 64],
                               bank[:, 65 * p:65 * p + 64],
                               rec_t[:, ridx:ridx + 1], None, ALU.mult))(),
                          waits=([("DV", v_rec), ("PE", ytm_free_pe)]
                                 if hh == 0 else []),
                          inc="DV" if hh == 3 else None)
                v_ytm = S.val("DV")
                ybank_free_dv = v_ytm
                tph = i % 2
                for p in range(2):
                    S.add("tensor",
                          (lambda p=p, tph=tph, i=i: lambda t: t.transpose(
                              tpP[tph][:, ((2 * i) % 4 + p) * 128:
                                       ((2 * i) % 4 + p + 1) * 128],
                              y_tm[:, p * 128:(p + 1) * 128], ident[:]))(),
                          waits=([("DV", v_ytm), ("DV", tp_free[tph])]
                                 if p == 0 else []),
                          inc="PE" if p == 1 else None)
                pe_ytp = S.val("PE")
                ytm_free_pe = pe_ytp
                for p in range(2):
                    wv = [("PE", pe_ytp)]
                    if i == 0:
                        wv.append(("sSt", ystrip_st[p]))
                    S.add("vector",
                          (lambda p=p, tph=tph, i=i: lambda v: v.tensor_copy(
                              y_strip[:, p * 512 + i * 128:p * 512 + (i + 1) * 128],
                              tpP[tph][:, ((2 * i) % 4 + p) * 128:
                                       ((2 * i) % 4 + p + 1) * 128]))(),
                          waits=wv, inc="DV" if p == 1 else None)
                tp_free[tph] = S.val("DV")
            y_ev = S.val("DV")
            if attn_sub == "sy":
                for p in range(2):
                    for tci4 in range(4):
                        S.add("sync",
                              (lambda g=g, p=p, sc=sc, tci4=tci4: lambda sy:
                               sy.dma_start(
                                   out=y_dram[2 * g + p, sc * 4 + tci4],
                                   in_=y_strip[:, p * 512 + tci4 * 128:
                                               p * 512 + (tci4 + 1) * 128]))(),
                              waits=[("DV", y_ev)] if tci4 == 0 else [],
                              inc="sSt", inc_amt=D16)
                    ystrip_st[p] = S.val("sSt")
                continue

            first = True
            for hh in range(4):
                p, par = hh // 2, hh % 2
                for j in range(4):
                    tk = sc * 4 + j
                    last = (hh == 3 and j == 3)
                    S.add("tensor",
                          (lambda p=p, par=par, tk=tk, hh=hh, st0=(j == 0),
                                  last=last: lambda t: t.matmul(
                              stbank[64 * par:64 * par + 64, 65 * p:65 * p + 65],
                              ek_strip[:, p * L + tk * 128 + 64 * par:
                                       p * L + tk * 128 + 64 * par + 64],
                              v65[:, tk * 260 + hh * 65:tk * 260 + (hh + 1) * 65],
                              start=st0, stop=last, skip_group_check=True))(),
                          waits=[("DV", st_ready)] if first else [],
                          inc="PE" if last else None)
                    first = False
            pe_st = S.val("PE")
            S.add("vector",
                  lambda v: v.tensor_add(st_f32[:, 0:130], st_f32[:, 0:130],
                                         stbank[:, 0:130]),
                  waits=[("PE", pe_st)], inc="DV")
            v_sta = S.val("DV")
            for hh in range(4):
                p, par = hh // 2, hh % 2
                S.add("vector",
                      (lambda p=p, par=par, hh=hh: lambda v: v.tensor_copy(
                          st_pad[64 * par:64 * par + 64, hh * 65:hh * 65 + 65],
                          st_f32[64 * par:64 * par + 64, 65 * p:65 * p + 65]))(),
                      waits=[("DV", v_sta)] if hh == 0 else [],
                      inc="DV" if hh == 3 else None)
            st_ready = S.val("DV")

            for p in range(2):
                for tci4 in range(4):
                    S.add("sync",
                          (lambda g=g, p=p, sc=sc, tci4=tci4: lambda sy:
                           sy.dma_start(
                               out=y_dram[2 * g + p, sc * 4 + tci4],
                               in_=y_strip[:, p * 512 + tci4 * 128:
                                           p * 512 + (tci4 + 1) * 128]))(),
                          waits=[("DV", y_ev)] if tci4 == 0 else [],
                          inc="sSt", inc_amt=D16)
                ystrip_st[p] = S.val("sSt")
        prev_group_pe = S.val("PE")
    y_all_st = S.val("sSt")
    for nm, buf in (("fkfm", fk_fm), ("fqfm", fq_fm), ("ek", ek_strip),
                    ("v65t", v65), ("rkt", rk)):
        t = tap(nm, [128, buf.shape[1]], BF16 if nm != "rkt" else F32)
        if t is not None:
            S.add("sync",
                  (lambda t=t, buf=buf: lambda sy: sy.dma_start(out=t, in_=buf[:]))(),
                  waits=[("DV", S.val("DV")), ("AC", S.val("AC")),
                         ("PE", S.val("PE"))],
                  inc="sSt", inc_amt=D16)
    S.flush()
    ph23.close()

    if stop_after <= 3:
        emit_scratch_taps([("w2p", w2p, [NHC, 128, 2048], BF16), ("y", y_dram, [NDC, NT, 128, 128], BF16)])
        S.add("sync", lambda sy: sy.nop(),
              waits=[("sSt", S.val("sSt")), ("sX", S.val("sX")),
                     ("gW", S.val("gW"))])
        S.flush()
        stack.close()
        return nc, tap_h

    ph4 = contextlib.ExitStack()
    woT_sb = sb(ph4, "woT_sb", [128, NDC * 2048], BF16)
    y_t = sb(ph4, "y_t", [128, 2048 * 2], BF16)
    ao_sb = sb(ph4, "ao_sb", [128, D], F32)
    x1_sb = sb(ph4, "x1_sb", [128, D], F32)
    ln2_tm = sb(ph4, "ln2_tm", [128, D], BF16)
    stg = sb(ph4, "stg", [128, 512 * 2], BF16)
    pW = [psum(ph4, f"pw{i}") for i in range(4)]
    tpP = [psum(ph4, "tp0b", BF16), psum(ph4, "tp1b", BF16)]
    tp_free = [0, 0]

    for dc in range(NDC):
        for dob in range(4):
            S.add("gpsimd",
                  (lambda dc=dc, dob=dob: lambda gp: gp.dma_start(
                      out=woT_sb[:, dc * 2048 + dob * 512:dc * 2048 + (dob + 1) * 512],
                      in_=woT[dc * 128:(dc + 1) * 128, dob * 512:(dob + 1) * 512]))(),
                  inc="gW", inc_amt=D16)
    g_wo = flush("gpsimd", "gW")

    y_used = [0, 0]
    x0_used = [0, 0]
    bank_free = [0] * 4
    stg_free = [0, 0]
    ao_free = 0
    x1st_prev = 0
    ln2tm_free = 0
    for tc in range(NT):
        slot = tc % 2
        for k in range(NDC):
            S.add("sync",
                  (lambda tc=tc, slot=slot, k=k: lambda sy: sy.dma_start(
                      out=y_t[:, slot * 2048 + k * 128:slot * 2048 + (k + 1) * 128],
                      in_=y_dram[k, tc]))(),
                  waits=([("sSt", y_all_st), ("PE", y_used[slot])]
                         if k == 0 else []),
                  inc="sX", inc_amt=D16)
        y_ld = flush("sync", "sX")
        S.add("sync",
              (lambda tc=tc, slot=slot: lambda sy: sy.dma_start(
                  out=x0_t[:, slot * D:(slot + 1) * D],
                  in_=x_in[tc * 128:(tc + 1) * 128, :]))(),
              waits=[("DV", x0_used[slot])], inc="sX", inc_amt=D16)
        x0_ld = flush("sync", "sX")
        for dob in range(4):
            bank = pW[dob]
            for k in range(NDC):
                S.add("tensor",
                      (lambda bank=bank, k=k, dob=dob, slot=slot: lambda t:
                       t.matmul(
                           bank[:, 0:512],
                           y_t[:, slot * 2048 + k * 128:slot * 2048 + (k + 1) * 128],
                           woT_sb[:, k * 2048 + dob * 512:k * 2048 + (dob + 1) * 512],
                           start=(k == 0), stop=(k == NDC - 1)))(),
                      waits=([("sX", y_ld), ("gW", g_wo), ("AC", bank_free[dob])]
                             if k == 0 else []),
                      inc="PE" if k == NDC - 1 else None)
            pe_wo = S.val("PE")
            S.add("scalar",
                  (lambda bank=bank, dob=dob: lambda s: s.copy(
                      ao_sb[:, dob * 512:(dob + 1) * 512], bank[:, 0:512]))(),
                  waits=[("PE", pe_wo)] + ([("DV", ao_free)] if dob == 0 else []),
                  inc="AC")
            bank_free[dob] = S.val("AC")
        y_used[slot] = S.val("PE")
        ac_ao = S.val("AC")
        vst = emit_ln_stats(lambda j: ao_sb[:, 512 * j:512 * (j + 1)],
                            [("AC", ac_ao)])
        S.add("vector",
              lambda v: v.tensor_scalar(ao_sb[:], ao_sb[:], ln_s1[:, 0:1], ln_s2[:, 0:1],
                                        ALU.add, ALU.mult),
              waits=[("DV", vst)])
        S.add("vector",
              (lambda slot=slot: lambda v: v.tensor_add(
                  x1_sb[:], ao_sb[:], x0_t[:, slot * D:(slot + 1) * D]))(),
              waits=[("sX", x0_ld), ("sSt", x1st_prev)], inc="DV")
        x0_used[slot] = S.val("DV")
        v_x1 = S.val("DV")
        ao_free = v_x1
        S.add("sync",
              (lambda tc=tc: lambda sy: sy.dma_start(
                  out=x1_dram[tc * 128:(tc + 1) * 128, :], in_=x1_sb[:]))(),
              waits=[("DV", v_x1)], inc="sSt", inc_amt=D16)
        x1st_prev = S.val("sSt")
        vst2 = emit_ln_stats(lambda j: x1_sb[:, 512 * j:512 * (j + 1)], [])
        S.add("vector",
              lambda v: v.tensor_scalar(ln2_tm[:], x1_sb[:], ln_s1[:, 0:1], ln_s2[:, 0:1],
                                        ALU.add, ALU.mult),
              waits=[("DV", vst2), ("PE", ln2tm_free)], inc="DV")
        v_ln2 = S.val("DV")
        for q4 in range(4):
            hp = q4 % 2
            for j in range(4):
                dc = q4 * 4 + j
                S.add("tensor",
                      (lambda dc=dc, hp=hp, j=j: lambda t: t.transpose(
                          tpP[hp][:, j * 128:(j + 1) * 128],
                          ln2_tm[:, dc * 128:(dc + 1) * 128], ident[:]))(),
                      waits=([("DV", v_ln2), ("DV", tp_free[hp])]
                             if j == 0 else []),
                      inc="PE" if j == 3 else None)
            pe_tp = S.val("PE")
            S.add("vector",
                  (lambda hp=hp: lambda v: v.tensor_copy(
                      stg[:, hp * 512:(hp + 1) * 512], tpP[hp][:, 0:512]))(),
                  waits=[("PE", pe_tp), ("sSt", stg_free[hp])], inc="DV")
            tp_free[hp] = S.val("DV")
            v_stg = S.val("DV")
            blk_i, ci = tc // 4, tc % 4
            for j4 in range(4):
                S.add("sync",
                      (lambda blk_i=blk_i, ci=ci, q4=q4, hp=hp, j4=j4:
                       lambda sy: sy.dma_start(
                           out=ln2_dram[blk_i, q4 * 4 + j4, :,
                                        ci * 128:(ci + 1) * 128],
                           in_=stg[:, hp * 512 + j4 * 128:
                                   hp * 512 + (j4 + 1) * 128]))(),
                      waits=[("DV", v_stg)] if j4 == 0 else [],
                      inc="sSt", inc_amt=D16)
            stg_free[hp] = S.val("sSt")
        ln2tm_free = S.val("PE")
    ln2_all_st = S.val("sSt")
    x1_all_st = ln2_all_st
    S.flush()
    ph4.close()

    if stop_after <= 4:
        emit_scratch_taps([("w2p", w2p, [NHC, 128, 2048], BF16), ("y", y_dram, [NDC, NT, 128, 128], BF16), ("x1", x1_dram, [L, D], F32), ("ln2", ln2_dram, [NBLK, NDC, 128, 512], BF16)])
        S.add("sync", lambda sy: sy.nop(),
              waits=[("sSt", S.val("sSt")), ("sX", S.val("sX")),
                     ("gW", S.val("gW"))])
        S.flush()
        stack.close()
        return nc, tap_h

    ph5 = contextlib.ExitStack()
    ln2_blk = sb(ph5, "ln2_blk", [128, NDC * 512], BF16)
    h_fm = sb(ph5, "h_fm", [128, NHC * 512], BF16)
    w13_sb = sb(ph5, "w13_sb", [128, 4096 * 3], BF16)
    w2p_sb = sb(ph5, "w2p_sb", [128, 1024 * 12], BF16)
    silu_sb = sb(ph5, "silu_sb", [128, 512 * 2], F32)
    fo_sb = sb(ph5, "fo_sb", [128, 4 * D], F32)
    x1_t = sb(ph5, "x1_t", [128, D * 2], F32)
    wecho = None
    pU = [psum(ph5, f"pu{i}") for i in range(8)]

    ln2_ld = 0
    ln2_used = 0
    w13_ld = [0, 0, 0]
    w13_used = [0, 0, 0]
    w2p_fl = {}
    w2p_used6 = [0] * 12
    w2p_emitted = set()
    silu_free = [0, 0]
    h_used = 0
    ac_fo_prev = 0
    out_st = 0
    x1_used = [0, 0]
    fo_free = [0] * 8
    last_out = 0
    def emit_w2p_load(j, blk_i):
        if j in w2p_emitted or j >= 128:
            return
        w2p_emitted.add(j)
        ws = j % 12
        phj, kj = divmod(j, 64)
        S.add("gpsimd",
              (lambda kj=kj, ws=ws, phj=phj: lambda e:
               e.dma_start(
                   out=w2p_sb[:, ws * 1024:(ws + 1) * 1024],
                   in_=w2p[kj][:, phj * 1024:(phj + 1) * 1024]))(),
              waits=[("PE", w2p_used6[ws])] +
                    ([("sSt", w2p_all_st)]
                     if (blk_i == 0 and j == 0) else []),
              inc="gW", inc_amt=D16)
        w2p_fl[j] = ("gW", flush("gpsimd", "gW"))

    for blk_i in range(NBLK):
        w2p_emitted.clear()
        for dc in range(NDC):
            S.add("sync",
                  (lambda blk_i=blk_i, dc=dc: lambda sy:
                   sy.dma_start(
                       out=ln2_blk[:, dc * 512:(dc + 1) * 512],
                       in_=ln2_dram[blk_i, dc]))(),
                  waits=([("sSt", ln2_all_st), ("PE", ln2_used)]
                         if dc == 0 else []),
                  inc="sX", inc_amt=D16)
        ln2_ld = flush("sync", "sX")

        for hc in range(NHC):
            ws = hc % 3
            for wi, wsrc in ((0, w1Tt), (1, w3Tt)):
                S.add("gpsimd",
                      (lambda hc=hc, ws=ws, wi=wi, wsrc=wsrc: lambda gp:
                       gp.dma_start(
                           out=w13_sb[:, ws * 4096 + wi * 2048:
                                      ws * 4096 + wi * 2048 + 2048],
                           in_=wsrc[hc].rearrange("p dc f -> p (dc f)")))(),
                      waits=[("PE", w13_used[ws])] if wi == 0 else [],
                      inc="gW", inc_amt=D16)
            w13_ld[ws] = flush("gpsimd", "gW")
            gb = pU[2 * (hc % 2)]
            ub = pU[2 * (hc % 2) + 1]
            for wi in range(2):
                bank = gb if wi == 0 else ub
                for dc in range(NDC):
                    S.add("tensor",
                          (lambda bank=bank, ws=ws, wi=wi, dc=dc:
                           lambda t: t.matmul(
                               bank[:, 0:512],
                               w13_sb[:, ws * 4096 + wi * 2048 + dc * 128:
                                      ws * 4096 + wi * 2048 + (dc + 1) * 128],
                               ln2_blk[:, dc * 512:(dc + 1) * 512],
                               start=(dc == 0), stop=(dc == NDC - 1)))(),
                          waits=([("gW", w13_ld[ws]), ("sX", ln2_ld),
                                  ("DV", silu_free[hc % 2]),
                                  ("AC", ac_fo_prev),
                                  ("PE", h_used)] if (dc == 0 and wi == 0)
                                 else []),
                          inc="PE" if dc == NDC - 1 else None)
            w13_used[ws] = S.val("PE")
            pe_up = S.val("PE")
            sslot = hc % 2
            S.add("scalar",
                  (lambda gb=gb, sslot=sslot: lambda s: s.activation(
                      silu_sb[:, sslot * 512:(sslot + 1) * 512], gb[:, 0:512],
                      AF.Silu, bias=0.0, scale=1.0))(),
                  waits=[("PE", pe_up), ("DV", silu_free[sslot])], inc="AC")
            ac_si = S.val("AC")
            S.add("vector",
                  (lambda ub=ub, sslot=sslot, hc=hc: lambda v: v.tensor_tensor(
                      h_fm[:, hc * 512:(hc + 1) * 512],
                      silu_sb[:, sslot * 512:(sslot + 1) * 512],
                      ub[:, 0:512], ALU.mult))(),
                  waits=[("AC", ac_si)], inc="DV")
            silu_free[hc % 2] = S.val("DV")
        h_ready = S.val("DV")
        ln2_used = S.val("PE")

        for ph in range(2):
            for k in range(NHC):
                kk = ph * 64 + k
                for j in range(kk, kk + 8):
                    emit_w2p_load(j, blk_i)
                ws = kk % 12
                for tci in range(4):
                    for dd in range(2):
                        bank = pU[tci * 2 + dd]
                        w_mm = []
                        if tci == 0 and dd == 0:
                            w_mm.append(w2p_fl[min(kk + 4, 127)])
                            if kk == 0:
                                w_mm.append(("DV", h_ready))
                        if k == 0:
                            w_mm.append(("AC", fo_free[tci * 2 + dd]))
                        S.add("tensor",
                              (lambda bank=bank, k=k, ws=ws, tci=tci, dd=dd,
                                      ph=ph: lambda t: t.matmul(
                                  bank[:, 0:512],
                                  h_fm[:, k * 512 + tci * 128:k * 512 + (tci + 1) * 128],
                                  w2p_sb[:, ws * 1024 + dd * 512:
                                         ws * 1024 + (dd + 1) * 512],
                                  start=(k == 0), stop=(k == NHC - 1)))(),
                              waits=w_mm,
                              inc="PE" if (tci == 3 and dd == 1) else None)
                w2p_used6[ws] = S.val("PE")
            pe_dn = S.val("PE")
            for tci in range(4):
                for dd in range(2):
                    bank = pU[tci * 2 + dd]
                    dob = 2 * ph + dd
                    w_ev = [("PE", pe_dn)]
                    if ph == 0 and tci == 0 and dd == 0 and blk_i > 0:
                        w_ev += [("DV", x1_used[0]), ("DV", x1_used[1]),
                                 ("sSt", last_out)]
                    S.add("scalar",
                          (lambda bank=bank, tci=tci, dob=dob: lambda s: s.copy(
                              fo_sb[:, tci * D + dob * 512:tci * D + (dob + 1) * 512],
                              bank[:, 0:512]))(),
                          waits=w_ev, inc="AC")
                    fo_free[tci * 2 + dd] = S.val("AC")
        ac_fo = S.val("AC")
        ac_fo_prev = ac_fo
        h_used = S.val("PE")
        if "fo" in taps and blk_i == NBLK - 1:
            t_fo = tap("fo", [128, 4 * D])
            S.add("sync",
                  (lambda t_fo=t_fo: lambda sy: sy.dma_start(
                      out=t_fo, in_=fo_sb[:]))(),
                  waits=[("AC", ac_fo)], inc="sSt", inc_amt=D16)
            fo_tap_st = S.val("sSt")
        else:
            fo_tap_st = 0

        for tci in range(4):
            tc = blk_i * 4 + tci
            xslot = tci % 2
            S.add("sync",
                  (lambda tc=tc, xslot=xslot: lambda sy: sy.dma_start(
                      out=x1_t[:, xslot * D:(xslot + 1) * D],
                      in_=x1_dram[tc * 128:(tc + 1) * 128, :]))(),
                  waits=[("sSt", x1_all_st), ("DV", x1_used[xslot])],
                  inc="sX", inc_amt=D16)
            x1_ld = flush("sync", "sX")
            vst = emit_ln_stats(
                (lambda tci=tci: lambda j:
                 fo_sb[:, tci * D + 512 * j:tci * D + 512 * (j + 1)])(),
                [("AC", ac_fo), ("sSt", fo_tap_st)])
            S.add("vector",
                  (lambda tci=tci: lambda v: v.tensor_scalar(
                      fo_sb[:, tci * D:(tci + 1) * D],
                      fo_sb[:, tci * D:(tci + 1) * D],
                      ln_s1[:, 0:1], ln_s2[:, 0:1], ALU.add, ALU.mult))(),
                  waits=[("DV", vst)])
            S.add("vector",
                  (lambda tci=tci, xslot=xslot: lambda v: v.tensor_add(
                      fo_sb[:, tci * D:(tci + 1) * D],
                      fo_sb[:, tci * D:(tci + 1) * D],
                      x1_t[:, xslot * D:(xslot + 1) * D]))(),
                  waits=[("sX", x1_ld)], inc="DV")
            x1_used[xslot] = S.val("DV")
            v_out = S.val("DV")
            S.add("sync",
                  (lambda tc=tc, tci=tci: lambda sy: sy.dma_start(
                      out=out[tc * 128:(tc + 1) * 128, :],
                      in_=fo_sb[:, tci * D:(tci + 1) * D]))(),
                  waits=[("DV", v_out)], inc="sSt", inc_amt=D16)
            last_out = S.val("sSt")

    t_h = tap("hfm", [128, NHC * 512], BF16)
    if t_h is not None:
        S.add("sync",
              lambda sy: sy.dma_start(out=t_h, in_=h_fm[:]),
              waits=[("DV", S.val("DV")), ("AC", S.val("AC")),
                     ("PE", S.val("PE"))], inc="sSt", inc_amt=D16)
    emit_scratch_taps([("w2p", w2p, [NHC, 128, 2048], BF16),
                       ("y", y_dram, [NDC, NT, 128, 128], BF16),
                       ("x1", x1_dram, [L, D], F32),
                       ("ln2", ln2_dram, [NBLK, NDC, 128, 512], BF16)])

    S.add("sync", lambda sy: sy.nop(), waits=[("sSt", S.val("sSt"))])
    S.flush()
    ph5.close()
    stack.close()
    return nc, tap_h


_PROG_CACHE = {}


def _get_prog(taps=()):
    key = tuple(sorted(taps))
    if key not in _PROG_CACHE:
        _PROG_CACHE[key] = build_program(taps=key)
    return _PROG_CACHE[key]


def prep_inputs(inputs):
    x = np.ascontiguousarray(np.asarray(inputs["L0_attn_hidden_states"]))
    wq = np.asarray(inputs["L0_attn_Wq"])
    wk = np.asarray(inputs["L0_attn_Wk"])
    wv = np.asarray(inputs["L0_attn_Wv"])
    wo = np.asarray(inputs["L0_attn_Wo"])
    wfq = np.ascontiguousarray(np.asarray(inputs["L0_attn_Wfq"]))
    wfk = np.ascontiguousarray(np.asarray(inputs["L0_attn_Wfk"]))
    w1 = np.asarray(inputs["L0_ffn_W1"])
    w2 = np.asarray(inputs["L0_ffn_W2"])
    w3 = np.asarray(inputs["L0_ffn_W3"])

    def tile13(w):
        wt = w.T
        return np.ascontiguousarray(
            wt.reshape(NDC, 128, NHC, 128).transpose(2, 1, 0, 3))

    common = {
        "wqT": np.ascontiguousarray(wq.T),
        "wkT": np.ascontiguousarray(wk.T),
        "wvT": np.ascontiguousarray(wv.T),
        "woT": np.ascontiguousarray(wo.T),
        "wfq": wfq, "wfk": wfk,
        "w1Tt": tile13(w1),
        "w3Tt": tile13(w3),
        "w2tp": np.ascontiguousarray(
            w2.T.reshape(64, 128, D).transpose(1, 0, 2).reshape(FF, D)),
    }
    common["pad_in"] = np.zeros((4096, 1024), np.float32)
    in_maps = []
    for b in range(B):
        m = dict(common)
        m["x_in"] = np.ascontiguousarray(x[b])
        in_maps.append(m)
    return in_maps


def kernel(**inputs):
    nc, _ = _get_prog()
    in_maps = prep_inputs(inputs)
    res = run_bass_kernel_spmd(nc, in_maps, core_ids=list(range(B)))
    return np.stack([np.asarray(res.results[b]["out"]) for b in range(B)], axis=0)
